# version 13
# baseline (speedup 1.0000x reference)
import math
import contextlib
import numpy as np
import concourse.bass as bass
import concourse.mybir as mybir
from concourse.bass_utils import run_bass_kernel_spmd

F32 = mybir.dt.float32
BF16 = mybir.dt.bfloat16
AF = mybir.ActivationFunctionType
ALU = mybir.AluOpType
AX = mybir.AxisListType

B, T, D = 4, 4096, 2048
NH, DH = 16, 128
HPC = 8
HALF = 1024
TOK = 2048
NQT = 8
QW = 512
SCALE = 1.0 / math.sqrt(DH)
CSTAB = 40.0
EPS = 1e-5
NCORE = 8
ENG = ["sync", "tensor", "vector", "scalar", "gpsimd"]


class Sem:
    def __init__(self, handle):
        self.h = handle
        self.n = 0

    def inc(self, k=1):
        self.n += k
        return self.n


class Rec:
    def __init__(self):
        self.ops = []

    def emit(self, fn):
        self.ops.append(fn)

    def run(self, eng):
        for fn in self.ops:
            fn(eng)


import os
STOP = int(os.environ.get("ATH_STOP", "99"))


def build_nc(mode="fused"):
    nc = bass.Bass("TRN2", target_bir_lowering=False)

    P12 = mode in ("fused", "l1")
    P3P = mode in ("fused", "l2")
    if P12:
        xb = nc.dram_tensor("xb", [T, D], F32, kind="ExternalInput").ap()
        wq_h = nc.dram_tensor("wq", [D, D], F32, kind="ExternalInput").ap()
        wk_h = nc.dram_tensor("wk", [D, HALF], F32, kind="ExternalInput").ap()
        wv_h = nc.dram_tensor("wv", [D, HALF], F32, kind="ExternalInput").ap()
        bq_h = nc.dram_tensor("bq", [128, 16], F32, kind="ExternalInput").ap()
        bk_h = nc.dram_tensor("bk", [128, HPC], F32, kind="ExternalInput").ap()
        bv_h = nc.dram_tensor("bv", [128, HPC], F32, kind="ExternalInput").ap()
        masks_h = nc.dram_tensor("masks", [128, 4 * QW], F32, kind="ExternalInput").ap()
        ones_h = nc.dram_tensor("ones128", [128, 128], F32, kind="ExternalInput").ap()
    xres = nc.dram_tensor("xres", [TOK, D], F32, kind="ExternalInput").ap()
    eye_h = nc.dram_tensor("eye", [128, 128], F32, kind="ExternalInput").ap()
    cvec_h = nc.dram_tensor("cvec", [128, 2], F32, kind="ExternalInput").ap()
    if P3P:
        wo_h = nc.dram_tensor("wo", [D, D], F32, kind="ExternalInput").ap()
        bob_h = nc.dram_tensor("bob", [128, D], F32, kind="ExternalInput").ap()
        w1t_h = nc.dram_tensor("w1t", [D, D], F32, kind="ExternalInput").ap()
        w3t_h = nc.dram_tensor("w3t", [D, D], F32, kind="ExternalInput").ap()
        wrt_h = nc.dram_tensor("wrt", [D, 8], F32, kind="ExternalInput").ap()
        out_h = nc.dram_tensor("out", [TOK, D], F32, kind="ExternalOutput").ap()
    if mode == "fused":
        w2t_h = nc.dram_tensor("w2t", [D, D], F32, kind="ExternalInput").ap()
        psel_h = nc.dram_tensor("psel", [128, 1], F32, kind="ExternalInput").ap()
        npsel_h = nc.dram_tensor("npsel", [128, 1], F32, kind="ExternalInput").ap()
        gate_h = nc.dram_tensor("gate", [128, 1], F32, kind="ExternalInput").ap()
    if mode == "l2":
        cm_h = nc.dram_tensor("cm_h", [HALF, TOK], BF16, kind="ExternalInput").ap()
        cp_h = nc.dram_tensor("cp_h", [HALF, TOK], BF16, kind="ExternalInput").ap()
        spre_h = nc.dram_tensor("spre", [8, D], F32, kind="ExternalOutput").ap()

    if P12:
        sq = nc.dram_tensor("sq", [D, TOK], BF16, kind="Internal").ap()
        sk = nc.dram_tensor("sk", [HALF, T], BF16, kind="Internal").ap()
        sv = nc.dram_tensor("sv", [HALF, T], BF16, kind="Internal").ap()
        if mode == "l1":
            cfull = nc.dram_tensor("cfull", [HALF, T], BF16, kind="ExternalOutput").ap()
        else:
            cfull = nc.dram_tensor("cfull", [HALF, T], BF16, kind="Internal").ap()
    if mode == "fused":
        csend = nc.dram_tensor("csend", [HALF, TOK], BF16, kind="Internal").ap()
        cmine_d = nc.dram_tensor("cmine_d", [HALF, TOK], BF16, kind="Internal").ap()
        agout = nc.dram_tensor("agout", [2 * HALF, TOK], BF16, kind="Internal").ap()
        ssum_in = nc.dram_tensor("ssum_in", [8, D], F32, kind="Internal").ap()
        ssum_out = nc.dram_tensor("ssum_out", [8, D], F32, kind="Internal").ap()
    if P3P:
        x1s = nc.dram_tensor("x1s", [TOK, D], BF16, kind="Internal").ap()
        sh1 = nc.dram_tensor("sh1", [D, TOK], BF16, kind="Internal").ap()

    R = {k: Rec() for k in ENG}
    cm = contextlib.ExitStack()

    def _finish():
        with nc.Block() as block:
            @block.sync
            def _(e):
                R["sync"].run(e)

            @block.tensor
            def _(e):
                R["tensor"].run(e)

            @block.vector
            def _(e):
                R["vector"].run(e)

            @block.scalar
            def _(e):
                R["scalar"].run(e)

            @block.gpsimd
            def _(e):
                R["gpsimd"].run(e)
        return nc


    def mksem(name):
        return Sem(cm.enter_context(nc.semaphore(name)))

    s_ld = mksem("s_ld")
    s_gld = mksem("s_gld")
    s_mm = mksem("s_mm")
    s_ve = mksem("s_ve")
    s_sc = mksem("s_sc")
    s_cc = mksem("s_cc")
    s_cst = mksem("s_cst")
    s_wo = mksem("s_wo")
    s_w1 = mksem("s_w1")
    s_w3 = mksem("s_w3")
    s_w2 = mksem("s_w2")
    s_st = mksem("s_st")
    s_ph = mksem("s_ph")
    sx = [mksem(f"sx{i}") for i in range(2)]
    sxs = [mksem(f"sxs{i}") for i in range(2)]
    sld = [mksem(f"sld{i}") for i in range(2)]
    scm = mksem("scm")
    sag = mksem("sag")
    ssp = [mksem(f"ssp{i}") for i in range(6)]

    def barrier():
        for e in ENG:
            R[e].emit(lambda g: g.nop().then_inc(s_ph.h, 1))
        tgt = s_ph.inc(len(ENG))
        for e in ENG:
            R[e].emit(lambda g, t=tgt: g.wait_ge(s_ph.h, t))

    sb = cm.enter_context

    eye_sb = sb(nc.sbuf_tensor("eye_sb", [128, 128], BF16))
    bq_sb = sb(nc.sbuf_tensor("bq_sb", [128, 16], F32))
    bk_sb = sb(nc.sbuf_tensor("bk_sb", [128, HPC], F32))
    bv_sb = sb(nc.sbuf_tensor("bv_sb", [128, HPC], F32))
    bob_sb = sb(nc.sbuf_tensor("bob_sb", [128, D], BF16))
    psel_sb = sb(nc.sbuf_tensor("psel_sb", [128, 1], F32))
    npsel_sb = sb(nc.sbuf_tensor("npsel_sb", [128, 1], F32))
    gate_sb = sb(nc.sbuf_tensor("gate_sb", [128, 1], F32))
    cv_sb = sb(nc.sbuf_tensor("cv_sb", [128, 2], F32))

    _cl = [(cv_sb, cvec_h)]
    if P12:
        _cl += [(bq_sb, bq_h), (bk_sb, bk_h), (bv_sb, bv_h)]
    if mode == "fused":
        _cl += [(psel_sb, psel_h), (npsel_sb, npsel_h), (gate_sb, gate_h)]

    def fn(s, _cl=_cl):
        for dst, src in _cl:
            s.dma_start(out=dst[:], in_=src[:, :]).then_inc(s_ld.h, 16)
    R["sync"].emit(fn)
    const_ld = s_ld.inc(len(_cl) * 16)

    _gl = [(eye_sb, eye_h)]
    if P3P:
        _gl += [(bob_sb, bob_h)]

    def fn(g, _gl=_gl):
        for dst, src in _gl:
            g.dma_start(out=dst[:], in_=src[:, :]).then_inc(s_cst.h, 16)
    R["gpsimd"].emit(fn)
    const_gld = s_cst.inc(len(_gl) * 16)

    def load_weight_cast(wsb, wh, ncols, tag, sem=None):
        if sem is None:
            sem = s_gld
        for c4 in range(4):
            def gn(g, wsb=wsb, wh=wh, c4=c4, ncols=ncols, sem=sem):
                dst = wsb[:, 4 * ncols * c4: 4 * ncols * (c4 + 1)].rearrange(
                    "p (c o) -> p c o", o=ncols)
                src = wh[512 * c4:512 * (c4 + 1), :].rearrange(
                    "(c p) o -> p c o", p=128)
                g.dma_start(out=dst, in_=src).then_inc(sem.h, 16)
            R["gpsimd"].emit(gn)
            sem.inc(16)
        return sem.n

    if not P12:
        pass
    else:
        p1 = contextlib.ExitStack()
        p1sb = p1.enter_context
        wq_sb = p1sb(nc.sbuf_tensor("wq_sb", [128, 16 * D], BF16))
        wk_sb = p1sb(nc.sbuf_tensor("wk_sb", [128, 16 * HALF], BF16))
        wv_sb = p1sb(nc.sbuf_tensor("wv_sb", [128, 16 * HALF], BF16))
        xbuf = [p1sb(nc.sbuf_tensor(f"xbuf{i}", [128, D], F32)) for i in range(2)]
        ybuf = [p1sb(nc.sbuf_tensor(f"ybuf{i}", [128, D], BF16)) for i in range(2)]
        stat = p1sb(nc.sbuf_tensor("stat", [128, 24], F32))
        mv = p1sb(nc.sbuf_tensor("mv", [128, 2], F32))
        sd = p1sb(nc.sbuf_tensor("sd", [128, 1], F32))
        rstd = p1sb(nc.sbuf_tensor("rstd", [128, 1], F32))
        y1t = [p1sb(nc.sbuf_tensor(f"y1t{i}", [128, 16 * QW], BF16)) for i in range(2)]
        qst = [p1sb(nc.sbuf_tensor(f"qst{i}", [128, QW], BF16)) for i in range(2)]
        kst = [p1sb(nc.sbuf_tensor(f"kst{i}", [128, QW], BF16)) for i in range(2)]
        vst = [p1sb(nc.sbuf_tensor(f"vst{i}", [128, QW], BF16)) for i in range(2)]

        psT = p1.enter_context(nc.psum_tensor("psT", [128, 1024], F32))
        psQ = [p1.enter_context(nc.psum_tensor(f"psQ{i}", [128, 512], F32)) for i in range(2)]
        psK = [p1.enter_context(nc.psum_tensor(f"psK{i}", [128, 512], F32)) for i in range(2)]
        psV = [p1.enter_context(nc.psum_tensor(f"psV{i}", [128, 512], F32)) for i in range(2)]

        load_weight_cast(wq_sb, wq_h, D, "wq")
        load_weight_cast(wk_sb, wk_h, HALF, "wk")
        wld_tok = load_weight_cast(wv_sb, wv_h, HALF, "wv")

        ln_done = {}
        ytr_done = {}
        tp_evac = {}
        y1t_free = [0, 0]
        stg_tok = {"q": [0, 0], "k": [0, 0], "v": [0, 0]}
        ps_evac = {"q": [0, 0], "k": [0, 0], "v": [0, 0]}

        for qt in range(NQT):
            y1 = qt % 2
            for s4 in range(4):
                t = 4 * qt + s4
                bi = t % 2

                def fn(s, t=t, bi=bi, w=ln_done.get(4 * qt + s4 - 2, 0)):
                    if w:
                        s.wait_ge(s_ve.h, w)
                    s.dma_start(out=xbuf[bi][:], in_=xb[128 * t:128 * (t + 1), :]
                                ).then_inc(sx[bi].h, 16)
                R["sync"].emit(fn)
                xw = sx[bi].inc(16)

                bnst_tok = s_ve.inc()
                bn_tok = s_ve.inc()

                def fn(v, t=t, bi=bi, xw=xw, lw=const_ld,
                       pw=ln_done.get(t - 1, 0), st=bnst_tok):
                    v.wait_ge(sx[bi].h, xw)
                    if t == 0:
                        v.wait_ge(s_ld.h, lw)
                    if pw:
                        v.wait_ge(s_ve.h, pw)
                    for g4 in range(4):
                        ins = v.bn_stats(stat[:, 6 * g4:6 * (g4 + 1)],
                                         xbuf[bi][:, 512 * g4:512 * (g4 + 1)])
                    ins.then_inc(s_ve.h, 1)
                    v.wait_ge(s_ve.h, st)
                    v.bn_aggr(mv[:], stat[:]).then_inc(s_ve.h, 1)
                R["vector"].emit(fn)

                def fn(sc, w=bn_tok):
                    sc.wait_ge(s_ve.h, w)
                    sc.activation(sd[:], mv[:, 1:2], AF.Sqrt, bias=cv_sb[:, 0:1]).then_inc(s_sc.h, 1)
                R["scalar"].emit(fn)
                sq_tok = s_sc.inc()

                rcp_tok = s_ve.inc()
                ln_done[t] = s_ve.inc()

                def fn(v, w=sq_tok, bi=bi, tw=ytr_done.get(t - 2, 0), rt=rcp_tok):
                    v.wait_ge(s_sc.h, w)
                    if tw:
                        v.wait_ge(s_mm.h, tw)
                    v.reciprocal(rstd[:], sd[:]).then_inc(s_ve.h, 1)
                    v.wait_ge(s_ve.h, rt)
                    v.tensor_scalar(ybuf[bi][:], xbuf[bi][:], mv[:, 0:1], rstd[:],
                                    ALU.subtract, ALU.mult).then_inc(s_ve.h, 1)
                R["vector"].emit(fn)

                for c in range(16):
                    slot = c % 2

                    def fn(tn, bi=bi, c=c, slot=slot, w=ln_done[t],
                           gw=const_gld if t == 0 and c == 0 else 0,
                           ew=tp_evac.get((t, c - 2), 0)):
                        if c == 0:
                            tn.wait_ge(s_ve.h, w)
                            if gw:
                                tn.wait_ge(s_cst.h, gw)
                        if ew:
                            tn.wait_ge(s_ve.h, ew)
                        tn.matmul(psT[:, 512 * slot:512 * slot + 128],
                                  ybuf[bi][:, 128 * c:128 * (c + 1)], eye_sb[:],
                                  start=True, stop=True).then_inc(s_mm.h, 1)
                    R["tensor"].emit(fn)
                    mm_tok = s_mm.inc()

                    def fn(v, w=mm_tok, c=c, slot=slot, s4=s4, y1=y1,
                           fw=y1t_free[y1] if (s4 == 0 and c == 0) else 0):
                        v.wait_ge(s_mm.h, w)
                        if fw:
                            v.wait_ge(s_mm.h, fw)
                        v.tensor_copy(
                            y1t[y1][:, 512 * c + 128 * s4:512 * c + 128 * (s4 + 1)],
                            psT[:, 512 * slot:512 * slot + 128]).then_inc(s_ve.h, 1)
                    R["vector"].emit(fn)
                    tp_evac[(t, c)] = s_ve.inc()
                ytr_done[t] = s_mm.n

            tpw = tp_evac[(4 * qt + 3, 15)]

            for kind, kbase, wsb, bias_sb, scale2, dst, pst, stg in (
                    ("k", 2, wk_sb, bk_sb, None, sk, psK, kst),
                    ("v", 4, wv_sb, bv_sb, None, sv, psV, vst)):
                for oc in range(HPC):
                    pi = oc % 2

                    def fn(tn, w=tpw, wg=wld_tok, oc=oc, pi=pi, wsb=wsb, y1=y1,
                           pst=pst, ew=ps_evac[kind][pi]):
                        if oc < 2:
                            tn.wait_ge(s_ve.h, w)
                            tn.wait_ge(s_gld.h, wg)
                        if ew:
                            tn.wait_ge(s_ve.h, ew)
                        for dc in range(16):
                            ins = tn.matmul(pst[pi][:],
                                      wsb[:, HALF * dc + 128 * oc:HALF * dc + 128 * (oc + 1)],
                                      y1t[y1][:, 512 * dc:512 * (dc + 1)],
                                      start=(dc == 0), stop=(dc == 15))
                        ins.then_inc(s_mm.h, 1)
                    R["tensor"].emit(fn)
                    mm_tok = s_mm.inc()

                    def fn(v, w=mm_tok, oc=oc, pi=pi, bias_sb=bias_sb, scale2=scale2,
                           pst=pst, stg=stg, sw=stg_tok[kind][pi], kb2=kbase):
                        v.wait_ge(s_mm.h, w)
                        if sw:
                            v.wait_ge(ssp[kb2 + pi].h, sw)
                        if scale2 is not None:
                            v.tensor_scalar(stg[pi][:], pst[pi][:],
                                            bias_sb[:, oc:oc + 1], scale2,
                                            ALU.add, ALU.mult).then_inc(s_ve.h, 1)
                        else:
                            v.tensor_scalar(stg[pi][:], pst[pi][:],
                                            bias_sb[:, oc:oc + 1], None,
                                            ALU.add).then_inc(s_ve.h, 1)
                    R["vector"].emit(fn)
                    ps_evac[kind][pi] = ev_tok = s_ve.inc()

                    def fn(s, w=ev_tok, oc=oc, pi=pi, qt=qt, dst=dst, stg=stg,
                           kb2=kbase):
                        s.wait_ge(s_ve.h, w)
                        s.dma_start(out=dst[128 * oc:128 * (oc + 1), QW * qt:QW * (qt + 1)],
                                    in_=stg[pi][:]).then_inc(ssp[kb2 + pi].h, 16)
                    R["scalar"].emit(fn)
                    stg_tok[kind][pi] = ssp[kbase + pi].inc(16)

            y1t_free[y1] = s_mm.n

        for qq in range(4):
            yq = qq % 2
            for s4 in range(4):
                t = 32 + 4 * qq + s4
                bi = t % 2
                tl = 4 * qq + s4

                def fn(s, tl=tl, bi=bi, w=ln_done.get(32 + 4 * qq + s4 - 2, 0)):
                    if w:
                        s.wait_ge(s_ve.h, w)
                    s.dma_start(out=xbuf[bi][:], in_=xres[128 * tl:128 * (tl + 1), :]
                                ).then_inc(sx[bi].h, 16)
                R["sync"].emit(fn)
                xw = sx[bi].inc(16)

                bnst_tok = s_ve.inc()
                bn_tok = s_ve.inc()

                def fn(v, bi=bi, xw=xw, pw=ln_done.get(t - 1, 0), st=bnst_tok):
                    v.wait_ge(sx[bi].h, xw)
                    if pw:
                        v.wait_ge(s_ve.h, pw)
                    for g4 in range(4):
                        ins = v.bn_stats(stat[:, 6 * g4:6 * (g4 + 1)],
                                         xbuf[bi][:, 512 * g4:512 * (g4 + 1)])
                    ins.then_inc(s_ve.h, 1)
                    v.wait_ge(s_ve.h, st)
                    v.bn_aggr(mv[:], stat[:]).then_inc(s_ve.h, 1)
                R["vector"].emit(fn)

                def fn(sc, w=bn_tok):
                    sc.wait_ge(s_ve.h, w)
                    sc.activation(sd[:], mv[:, 1:2], AF.Sqrt, bias=cv_sb[:, 0:1]).then_inc(s_sc.h, 1)
                R["scalar"].emit(fn)
                sq_tok = s_sc.inc()

                rcp_tok = s_ve.inc()
                ln_done[t] = s_ve.inc()

                def fn(v, w=sq_tok, bi=bi, tw=ytr_done.get(t - 2, 0), rt=rcp_tok):
                    v.wait_ge(s_sc.h, w)
                    if tw:
                        v.wait_ge(s_mm.h, tw)
                    v.reciprocal(rstd[:], sd[:]).then_inc(s_ve.h, 1)
                    v.wait_ge(s_ve.h, rt)
                    v.tensor_scalar(ybuf[bi][:], xbuf[bi][:], mv[:, 0:1], rstd[:],
                                    ALU.subtract, ALU.mult).then_inc(s_ve.h, 1)
                R["vector"].emit(fn)

                for c in range(16):
                    slot = c % 2

                    def fn(tn, bi=bi, c=c, slot=slot, w=ln_done[t],
                           ew=tp_evac.get((t, c - 2), 0)):
                        if c == 0:
                            tn.wait_ge(s_ve.h, w)
                        if ew:
                            tn.wait_ge(s_ve.h, ew)
                        tn.matmul(psT[:, 512 * slot:512 * slot + 128],
                                  ybuf[bi][:, 128 * c:128 * (c + 1)], eye_sb[:],
                                  start=True, stop=True).then_inc(s_mm.h, 1)
                    R["tensor"].emit(fn)
                    mm_tok = s_mm.inc()

                    def fn(v, w=mm_tok, c=c, slot=slot, s4=s4, yq=yq,
                           fw=y1t_free[yq] if (s4 == 0 and c == 0) else 0):
                        v.wait_ge(s_mm.h, w)
                        if fw:
                            v.wait_ge(s_mm.h, fw)
                        v.tensor_copy(
                            y1t[yq][:, 512 * c + 128 * s4:512 * c + 128 * (s4 + 1)],
                            psT[:, 512 * slot:512 * slot + 128]).then_inc(s_ve.h, 1)
                    R["vector"].emit(fn)
                    tp_evac[(t, c)] = s_ve.inc()
                ytr_done[t] = s_mm.n

            tpwq = tp_evac[(32 + 4 * qq + 3, 15)]
            for oc in range(16):
                pi = oc % 2

                def fn(tn, w=tpwq, oc=oc, pi=pi, yq=yq,
                       ew=ps_evac["q"][pi]):
                    if oc < 2:
                        tn.wait_ge(s_ve.h, w)
                    if ew:
                        tn.wait_ge(s_ve.h, ew)
                    for dc in range(16):
                        ins = tn.matmul(psQ[pi][:],
                                  wq_sb[:, D * dc + 128 * oc:D * dc + 128 * (oc + 1)],
                                  y1t[yq][:, 512 * dc:512 * (dc + 1)],
                                  start=(dc == 0), stop=(dc == 15))
                    ins.then_inc(s_mm.h, 1)
                R["tensor"].emit(fn)
                mm_tok = s_mm.inc()

                def fn(v, w=mm_tok, oc=oc, pi=pi, sw=stg_tok["q"][pi]):
                    v.wait_ge(s_mm.h, w)
                    if sw:
                        v.wait_ge(ssp[pi].h, sw)
                    v.tensor_scalar(qst[pi][:], psQ[pi][:],
                                    bq_sb[:, oc:oc + 1], SCALE,
                                    ALU.add, ALU.mult).then_inc(s_ve.h, 1)
                R["vector"].emit(fn)
                ps_evac["q"][pi] = ev_tok = s_ve.inc()

                def fn(s, w=ev_tok, oc=oc, pi=pi, qq=qq):
                    s.wait_ge(s_ve.h, w)
                    s.dma_start(out=sq[128 * oc:128 * (oc + 1), QW * qq:QW * (qq + 1)],
                                in_=qst[pi][:]).then_inc(ssp[pi].h, 16)
                R["scalar"].emit(fn)
                stg_tok["q"][pi] = ssp[pi].inc(16)

            y1t_free[yq] = s_mm.n

        def fn(s, ws=[(ssp[i].h, ssp[i].n) for i in range(6)]):
            for h, w in ws:
                if w:
                    s.wait_ge(h, w)
        R["sync"].emit(fn)
        barrier()
        p1.close()
        if STOP <= 1:
            cm.close()
            return _finish()

        p2 = contextlib.ExitStack()
        p2sb = p2.enter_context
        masks_sb = p2sb(nc.sbuf_tensor("masks_sb", [128, 4 * QW], BF16))
        ones_sb = p2sb(nc.sbuf_tensor("ones_sb", [128, 128], BF16))
        kstrip = [p2sb(nc.sbuf_tensor(f"kstrip{i}", [128, T], BF16)) for i in range(2)]
        vstrip = [p2sb(nc.sbuf_tensor(f"vstrip{i}", [128, T], BF16)) for i in range(2)]
        vstr_dh = [p2sb(nc.sbuf_tensor(f"vstr_dh{i}", [128, T], BF16)) for i in range(2)]
        qtile = [p2sb(nc.sbuf_tensor(f"qtile{i}", [128, QW], BF16)) for i in range(2)]
        eacc = [p2sb(nc.sbuf_tensor(f"eacc{i}", [128, QW], F32)) for i in range(2)]
        eaccb = [p2sb(nc.sbuf_tensor(f"eaccb{i}", [128, QW], BF16)) for i in range(2)]
        qslab = [p2sb(nc.sbuf_tensor(f"qslab{i}", [128, 16 * 256], BF16)) for i in range(2)]
        ebuf = [p2sb(nc.sbuf_tensor(f"ebuf{i}", [128, 32 * QW], BF16)) for i in range(2)]
        lnl = p2sb(nc.sbuf_tensor("lnl", [128, QW], F32))
        rwide = p2sb(nc.sbuf_tensor("rwide", [128, QW], F32))
        ctxstg = [p2sb(nc.sbuf_tensor(f"ctxstg{i}", [128, QW], BF16)) for i in range(2)]

        psS = p2.enter_context(nc.psum_tensor("psS", [128, 2048], F32))
        psL = p2.enter_context(nc.psum_tensor("psL", [128, 512], F32))
        psC = p2.enter_context(nc.psum_tensor("psC", [128, 512], F32))
        psT2 = p2.enter_context(nc.psum_tensor("psT2", [128, 1024], F32))

        def fn(g):
            g.dma_start(out=ones_sb[:], in_=ones_h[:, :]).then_inc(s_cst.h, 16)
            g.dma_start(out=masks_sb[:], in_=masks_h[:, :]).then_inc(s_cst.h, 16)
        R["gpsimd"].emit(fn)
        cst2 = s_cst.inc(32)

        it = 0
        ld_tok = {}
        vtr_evac = {}
        vtr_done = {}
        exp_tok = {}
        pv_tok = {}
        r_tok = {}
        ctx_tok = {}
        ctx_sp = {}
        qcp_tok = {}
        sqs = [mksem(f"sqs{i}") for i in range(2)]
        sks = [mksem(f"sks{i}") for i in range(2)]
        svs = [mksem(f"svs{i}") for i in range(2)]

        qs_toks, ks_toks, vs_toks, vtr_tok_h, psS_last, psL_rd = {}, {}, {}, {}, {}, {}

        def emit_head_loads(h):
            hbq = h % 2
            prev_last = (h - 2) * NQT + (NQT - 1)

            def fn(s, h=h, hbq=hbq, w=qcp_tok.get((h - 2, NQT - 1), 0),
                   pw=pv_tok.get(prev_last, 0)):
                if w:
                    s.wait_ge(s_ve.h, w)
                srcq = sq[:, 256 * h:256 * (h + 1)].rearrange(
                    "(c p) u -> p c u", p=128)
                dstq = qslab[hbq][:].rearrange("p (c u) -> p c u", u=256)
                s.dma_start(out=dstq, in_=srcq).then_inc(sqs[hbq].h, 16)
                if pw:
                    s.wait_ge(s_mm.h, pw)
                s.dma_start(out=kstrip[hbq][:],
                            in_=sk[128 * h:128 * (h + 1), :]).then_inc(sks[hbq].h, 16)
            R["sync"].emit(fn)
            qs_toks[h] = sqs[hbq].inc(16)
            ks_toks[h] = sks[hbq].inc(16)

            def fn(g, h=h, hbq=hbq, tw=vtr_done.get(h - 2, 0)):
                if tw:
                    g.wait_ge(s_mm.h, tw)
                g.dma_start(out=vstr_dh[hbq][:],
                            in_=sv[128 * h:128 * (h + 1), :]).then_inc(svs[hbq].h, 16)
            R["gpsimd"].emit(fn)
            vs_toks[h] = svs[hbq].inc(16)

        def emit_qcp(nit):
            hq, qtq = divmod(nit, NQT)
            hbq, biq = hq % 2, nit % 2
            qcp_tok[(hq, qtq)] = s_ve.inc()

            def fn(v, w=psS_last.get(nit - 2, 0),
                   qw=qs_toks[hq] if qtq == 0 else 0,
                   hbq=hbq, biq=biq, qtq=qtq):
                if qw:
                    v.wait_ge(sqs[hbq].h, qw)
                if w:
                    v.wait_ge(s_mm.h, w)
                srcv = qslab[hbq][:].rearrange("p (j u) -> p u j", j=16)[
                    :, 32 * qtq:32 * (qtq + 1), :]
                dstv = qtile[biq][:].rearrange("p (u j) -> p u j", j=16)
                v.tensor_copy(dstv, srcv).then_inc(s_ve.h, 1)
            R["vector"].emit(fn)

        def emit_tail(pit, all_exp_p, mask_tok_p, acc_fin_p):
            hq, qtq = divmod(pit, NQT)
            hbq, biq = hq % 2, pit % 2
            kbm_p = 4 * (qtq + 1)

            def fn(tn, w1=acc_fin_p, kbm=kbm_p, bi=biq, hb=hbq,
                   vw=vtr_tok_h[hq] if qtq == 0 else 0,
                   w2=mask_tok_p,
                   lw=r_tok.get(pit - 1, 0), cw2=ctx_tok.get(pit - 1, 0),
                   cw=cst2 if pit == 0 else 0):
                if cw:
                    tn.wait_ge(s_cst.h, cw)
                if vw:
                    tn.wait_ge(s_ve.h, vw)
                tn.wait_ge(s_ve.h, w1)
                if lw:
                    tn.wait_ge(s_sc.h, lw)
                tn.matmul(psL[:], ones_sb[:], eaccb[bi][:],
                          start=True, stop=True).then_inc(s_mm.h, 1)
                if cw2:
                    tn.wait_ge(s_ve.h, cw2)
                for kb in range(kbm):
                    ins = tn.matmul(psC[:], vstrip[hb][:, 128 * kb:128 * (kb + 1)],
                              ebuf[bi][:, QW * kb:QW * (kb + 1)],
                              start=(kb == 0), stop=(kb == kbm - 1))
                ins.then_inc(s_mm.h, 1)
            R["tensor"].emit(fn)
            psL_rd[pit] = s_mm.inc()
            pv_tok[pit] = s_mm.inc()

            ln_mid = s_sc.inc()
            r_tok[pit] = s_sc.inc()

            def fn(sc, w=pv_tok[pit], vw=ctx_tok.get(pit - 1, 0), lt=ln_mid):
                sc.wait_ge(s_mm.h, w)
                if vw:
                    sc.wait_ge(s_ve.h, vw)
                sc.activation(lnl[:], psL[:], AF.Ln).then_inc(s_sc.h, 1)
                sc.wait_ge(s_sc.h, lt)
                sc.activation(rwide[:], lnl[:], AF.Exp, scale=-1.0).then_inc(s_sc.h, 1)
            R["scalar"].emit(fn)

            def fn(v, w=r_tok[pit], bi=biq, sw=ctx_sp.get(pit - 2, 0)):
                v.wait_ge(s_sc.h, w)
                if sw:
                    v.wait_ge(ssp[bi].h, sw)
                v.tensor_tensor(ctxstg[bi][:], psC[:], rwide[:],
                                ALU.mult).then_inc(s_ve.h, 1)
            R["vector"].emit(fn)
            ctx_tok[pit] = s_ve.inc()

            def fn(s, w=ctx_tok[pit], hq=hq, qtq=qtq, bi=biq):
                s.wait_ge(s_ve.h, w)
                s.dma_start(out=cfull[128 * hq:128 * (hq + 1), QW * qtq:QW * (qtq + 1)],
                            in_=ctxstg[bi][:]).then_inc(ssp[bi].h, 16)
            R["sync"].emit(fn)
            ctx_sp[pit] = ssp[biq].inc(16)

        emit_head_loads(0)
        emit_qcp(0)
        pending = None

        for hh in range(HPC):
            hb = hh % 2
            prev_last = (hh - 2) * NQT + (NQT - 1)
            vs_tok = vs_toks[hh] if hh in vs_toks else None
            if vs_tok is None:
                vs_tok = vs_toks[0]

            for kb2 in range(32):
                slot2 = kb2 % 2

                def fn(tn, kb2=kb2, slot2=slot2, hb=hb, w=vs_tok,
                       pw=pv_tok.get(prev_last, 0),
                       ew=vtr_evac.get((hh, kb2 - 2), 0)):
                    if kb2 == 0:
                        tn.wait_ge(svs[hb].h, w)
                        if pw:
                            tn.wait_ge(s_mm.h, pw)
                    if ew:
                        tn.wait_ge(s_ve.h, ew)
                    tn.matmul(psT2[:, 512 * slot2:512 * slot2 + 128],
                              vstr_dh[hb][:, 128 * kb2:128 * (kb2 + 1)], eye_sb[:],
                              start=True, stop=True).then_inc(s_mm.h, 1)
                R["tensor"].emit(fn)
                vmm_tok = s_mm.inc()

                def fn(v, w=vmm_tok, kb2=kb2, slot2=slot2, hb=hb):
                    v.wait_ge(s_mm.h, w)
                    v.tensor_copy(vstrip[hb][:, 128 * kb2:128 * (kb2 + 1)],
                                  psT2[:, 512 * slot2:512 * slot2 + 128]).then_inc(s_ve.h, 1)
                R["vector"].emit(fn)
                vtr_evac[(hh, kb2)] = s_ve.inc()
            vtr_done[hh] = s_mm.n
            vtr_tok_h[hh] = vtr_evac[(hh, 31)]

            for qt in range(NQT):
                kbm = 4 * (qt + 1)
                bi = it % 2

                pmax_prev = 4 * ((qt - 1) + 1) // 2 if qt > 0 else 4 * NQT // 2
                for p in range(kbm // 2):
                    grp = p % 2

                    def fn(tn, p=p, grp=grp, bi=bi, hb=hb,
                           kw=ks_toks[hh] if qt == 0 else 0,
                           qw=qcp_tok[(hh, qt)],
                           ew=exp_tok.get((it, p - 2), 0) or
                              (exp_tok.get((it - 1,
                                            pmax_prev - 2 + (pmax_prev + p) % 2), 0)
                               if p < 2 else 0)):
                        if p == 0:
                            if kw:
                                tn.wait_ge(sks[hb].h, kw)
                            tn.wait_ge(s_ve.h, qw)
                        if ew:
                            tn.wait_ge(s_sc.h, ew)
                        for j in range(2):
                            kb = 2 * p + j
                            ins = tn.matmul(psS[:, 1024 * grp + 512 * j:1024 * grp + 512 * (j + 1)],
                                      kstrip[hb][:, 128 * kb:128 * (kb + 1)],
                                      qtile[bi][:], start=True, stop=True)
                        ins.then_inc(s_mm.h, 1)
                    R["tensor"].emit(fn)
                    mm_tok = s_mm.inc()

                    def fn(sc, w=mm_tok, p=p, grp=grp, bi=bi,
                           evw=pv_tok.get(it - 2, 0)):
                        sc.wait_ge(s_mm.h, w)
                        if p == 0 and evw:
                            sc.wait_ge(s_mm.h, evw)
                        sc.activation(ebuf[bi][:, QW * 2 * p:QW * 2 * (p + 1)],
                                      psS[:, 1024 * grp:1024 * (grp + 1)],
                                      AF.Exp, bias=cv_sb[:, 1:2]).then_inc(s_sc.h, 1)
                    R["scalar"].emit(fn)
                    exp_tok[(it, p)] = s_sc.inc()
                psS_last[it] = mm_tok

                all_exp = s_sc.n

                prev_acc = 0
                for kb in range(kbm):
                    j = kb - 4 * qt
                    mt = s_ve.inc() if j >= 0 else 0
                    at = s_ve.inc()

                    def fn(v, w=exp_tok[(it, kb // 2)], kb=kb, j=j, bi=bi,
                           pa=prev_acc, kbm=kbm, mt=mt,
                           pw=psL_rd.get(it - 2, 0) if kb == 0 else 0,
                           cw=cst2 if it == 0 else 0):
                        if cw:
                            v.wait_ge(s_cst.h, cw)
                        v.wait_ge(s_sc.h, w)
                        if j >= 0:
                            v.tensor_tensor(ebuf[bi][:, QW * kb:QW * (kb + 1)],
                                            ebuf[bi][:, QW * kb:QW * (kb + 1)],
                                            masks_sb[:, QW * j:QW * (j + 1)],
                                            ALU.mult).then_inc(s_ve.h, 1)
                            v.wait_ge(s_ve.h, mt)
                        if pw:
                            v.wait_ge(s_mm.h, pw)
                        if pa:
                            v.wait_ge(s_ve.h, pa)
                        if kb == 0:
                            v.tensor_copy(eacc[bi][:],
                                          ebuf[bi][:, 0:QW]).then_inc(s_ve.h, 1)
                        else:
                            dst = eaccb[bi][:] if kb == kbm - 1 else eacc[bi][:]
                            v.tensor_tensor(dst, eacc[bi][:],
                                            ebuf[bi][:, QW * kb:QW * (kb + 1)],
                                            ALU.add).then_inc(s_ve.h, 1)
                    R["vector"].emit(fn)
                    prev_acc = at
                mask_tok = prev_acc
                acc_fin = prev_acc

                nit = it + 1
                if nit < HPC * NQT:
                    if nit % NQT == 0:
                        emit_head_loads(nit // NQT)
                    emit_qcp(nit)
                if pending is not None:
                    emit_tail(*pending)
                pending = (it, all_exp, mask_tok, acc_fin)

                it += 1
        emit_tail(*pending)

        def fn(s, ws=[(ssp[i].h, ssp[i].n) for i in range(2)]):
            for h, w in ws:
                s.wait_ge(h, w)
        R["sync"].emit(fn)
        barrier()
        p2.close()
        if STOP <= 2 or mode == "l1":
            cm.close()
            return _finish()


    gall = sb(nc.sbuf_tensor("gall", [128, 16 * 8], BF16))
    spre_f = sb(nc.sbuf_tensor("spre_f", [8, D], F32))
    spre_b = sb(nc.sbuf_tensor("spre_b", [8, D], BF16))

    if mode == "fused":
        p25 = contextlib.ExitStack()
        cblk = [p25.enter_context(nc.sbuf_tensor(f"cblk{i}", [128, T], BF16)) for i in range(2)]
        cbt = [p25.enter_context(nc.sbuf_tensor(f"cbt{i}", [128, TOK], BF16)) for i in range(2)]
        cbm = [p25.enter_context(nc.sbuf_tensor(f"cbm{i}", [128, TOK], BF16)) for i in range(2)]
        cbs = [p25.enter_context(nc.sbuf_tensor(f"cbs{i}", [128, TOK], BF16)) for i in range(2)]

        bl_tok = {}
        bsp_tok = {}
        for hc in range(HPC):
            bi = hc % 2

            def fn(s, hc=hc, bi=bi, w=bl_tok.get(hc - 2, 0)):
                if w:
                    s.wait_ge(s_ve.h, w)
                s.dma_start(out=cblk[bi][:],
                            in_=cfull[128 * hc:128 * (hc + 1), :]).then_inc(sld[bi].h, 16)
            R["sync"].emit(fn)
            ld = sld[bi].inc(16)

            t1 = s_ve.inc(); t2 = s_ve.inc(); t3 = s_ve.inc()
            bl_tok[hc] = s_ve.inc()

            def fn(v, w=ld, bi=bi, sw=bsp_tok.get(hc - 2, 0), t1=t1, t2=t2, t3=t3):
                v.wait_ge(sld[bi].h, w)
                if sw:
                    v.wait_ge(ssp[2 + bi].h, sw)
                v.tensor_scalar_mul(cbt[bi][:], cblk[bi][:, 0:TOK],
                                    npsel_sb[:, 0:1]).then_inc(s_ve.h, 1)
                v.wait_ge(s_ve.h, t1)
                v.scalar_tensor_tensor(cbm[bi][:], cblk[bi][:, TOK:T], psel_sb[:, 0:1],
                                       cbt[bi][:], ALU.mult, ALU.add).then_inc(s_ve.h, 1)
                v.wait_ge(s_ve.h, t2)
                v.tensor_scalar_mul(cbt[bi][:], cblk[bi][:, 0:TOK],
                                    psel_sb[:, 0:1]).then_inc(s_ve.h, 1)
                v.wait_ge(s_ve.h, t3)
                v.scalar_tensor_tensor(cbs[bi][:], cblk[bi][:, TOK:T], npsel_sb[:, 0:1],
                                       cbt[bi][:], ALU.mult, ALU.add).then_inc(s_ve.h, 1)
            R["vector"].emit(fn)

            def fn(s, w=bl_tok[hc], hc=hc, bi=bi):
                s.wait_ge(s_ve.h, w)
                s.dma_start(out=cmine_d[128 * hc:128 * (hc + 1), :],
                            in_=cbm[bi][:]).then_inc(ssp[2 + bi].h, 16)
                s.dma_start(out=csend[128 * hc:128 * (hc + 1), :],
                            in_=cbs[bi][:]).then_inc(ssp[2 + bi].h, 16)
            R["sync"].emit(fn)
            bsp_tok[hc] = ssp[2 + bi].inc(32)

        def fn(g, w2=ssp[2].n, w3=ssp[3].n):
            g.wait_ge(ssp[2].h, w2)
            g.wait_ge(ssp[3].h, w3)
            g.collective_compute(
                "AllGather", ALU.bypass,
                replica_groups=[[0, 1], [2, 3], [4, 5], [6, 7]],
                ins=[csend[:, :]],
                outs=[agout[:, :]],
            ).then_inc(s_cc.h, 16)
        R["gpsimd"].emit(fn)
        ag_tok = s_cc.inc(16)
        barrier()
        p25.close()
        if STOP <= 3:
            cm.close()
            return _finish()


    p3 = contextlib.ExitStack()
    p3sb = p3.enter_context
    wo_sb = p3sb(nc.sbuf_tensor("wo_sb", [128, 16 * D], BF16))
    cm_sb = p3sb(nc.sbuf_tensor("cm_sb", [128, 8 * TOK], BF16))
    cp_sb = p3sb(nc.sbuf_tensor("cp_sb", [128, 8 * TOK], BF16))
    aga = [p3sb(nc.sbuf_tensor("aga0", [128, TOK], BF16))]
    agb = [p3sb(nc.sbuf_tensor("agb0", [128, TOK], BF16))]
    attmp = p3sb(nc.sbuf_tensor("attmp", [128, D], F32))
    x1tl = [p3sb(nc.sbuf_tensor(f"x1tl{i}", [128, D], F32)) for i in range(2)]
    xrb = [p3sb(nc.sbuf_tensor(f"xrb{i}", [128, D], F32)) for i in range(2)]
    stat3 = p3sb(nc.sbuf_tensor("stat3", [128, 24], F32))
    mv3 = p3sb(nc.sbuf_tensor("mv3", [128, 2], F32))
    sd3 = p3sb(nc.sbuf_tensor("sd3", [128, 1], F32))
    rstd3 = p3sb(nc.sbuf_tensor("rstd3", [128, 1], F32))
    psO = [p3.enter_context(nc.psum_tensor(f"psO{i}", [128, 2048], F32))
           for i in range(2)]

    wo_tok = load_weight_cast(wo_sb, wo_h, D, "wo", s_wo)

    _cm_src = cmine_d if mode == "fused" else cm_h

    def fn(s, w2=ssp[2].n, w3=ssp[3].n, _cm_src=_cm_src):
        s.wait_ge(ssp[2].h, w2)
        s.wait_ge(ssp[3].h, w3)
        src = _cm_src[:, :].rearrange("(c p) q -> p c q", p=128)
        dst = cm_sb[:].rearrange("p (c q) -> p c q", q=TOK)
        s.dma_start(out=dst, in_=src).then_inc(scm.h, 16)
    R["sync"].emit(fn)
    cmld_tok = scm.inc(16)

    if mode == "l2":
        def fn(s):
            srcp = cp_h[:, :].rearrange("(c p) q -> p c q", p=128)
            dstp = cp_sb[:].rearrange("p (c q) -> p c q", q=TOK)
            s.dma_start(out=dstp, in_=srcp).then_inc(sag.h, 16)
        R["sync"].emit(fn)
        cp_dma = sag.inc(16)
        ag_tok = 0

    pb_tok = {}
    for hc in range(HPC if mode == "fused" else 0):
        bi = 0

        def fn(s, hc=hc, bi=bi, w=ag_tok, bw=pb_tok.get(hc - 1, 0)):
            if hc == 0:
                s.wait_ge(s_cc.h, w)
            if bw:
                s.wait_ge(s_ve.h, bw)
            s.dma_start(out=aga[bi][:],
                        in_=agout[128 * hc:128 * (hc + 1), :]).then_inc(sag.h, 16)
            s.dma_start(out=agb[bi][:],
                        in_=agout[HALF + 128 * hc:HALF + 128 * (hc + 1), :]
                        ).then_inc(sag.h, 16)
        R["sync"].emit(fn)
        ld = sag.inc(32)

        ta = s_ve.inc()
        pb_tok[hc] = s_ve.inc()

        def fn(v, w=ld, hc=hc, bi=bi, ta=ta):
            v.wait_ge(sag.h, w)
            v.tensor_scalar_mul(aga[bi][:], aga[bi][:],
                                psel_sb[:, 0:1]).then_inc(s_ve.h, 1)
            v.wait_ge(s_ve.h, ta)
            v.scalar_tensor_tensor(cp_sb[:, TOK * hc:TOK * (hc + 1)], agb[bi][:],
                                   npsel_sb[:, 0:1], aga[bi][:],
                                   ALU.mult, ALU.add).then_inc(s_ve.h, 1)
        R["vector"].emit(fn)
    cp_tok = pb_tok[HPC - 1] if mode == "fused" else 0

    x1_done = {}
    x1_sp = {}
    o_evac = {}
    for tq in range(16):
        bi = tq % 2

        def fn(s, tq=tq, bi=bi, w=x1_done.get(tq - 2, 0)):
            if w:
                s.wait_ge(s_ve.h, w)
            s.dma_start(out=xrb[bi][:],
                        in_=xres[128 * tq:128 * (tq + 1), :]).then_inc(sx[bi].h, 16)
        R["sync"].emit(fn)
        xw = sx[bi].inc(16)

        def fn(tn, tq=tq, w1=cmld_tok, w2=cp_tok, w3=wo_tok,
               w4=cp_dma if mode == "l2" else 0,
               ew=o_evac.get(tq - 2, 0)):
            if tq == 0:
                tn.wait_ge(scm.h, w1)
                if w2:
                    tn.wait_ge(s_ve.h, w2)
                if w4:
                    tn.wait_ge(sag.h, w4)
                tn.wait_ge(s_wo.h, w3)
            if ew:
                tn.wait_ge(s_ve.h, ew)
            for dmb in range(4):
                for hcx in range(16):
                    src = cm_sb if hcx < 8 else cp_sb
                    hcl = hcx % 8
                    ins = tn.matmul(psO[tq % 2][:, 512 * dmb:512 * (dmb + 1)],
                              src[:, TOK * hcl + 128 * tq:TOK * hcl + 128 * (tq + 1)],
                              wo_sb[:, D * hcx + 512 * dmb:D * hcx + 512 * (dmb + 1)],
                              start=(hcx == 0), stop=(hcx == 15))
            ins.then_inc(s_mm.h, 1)
        R["tensor"].emit(fn)
        mm_tok = s_mm.inc()

        tadd = s_ve.inc(); tstat = s_ve.inc()
        o_evac[tq] = ve_tok = s_ve.inc()

        def fn(v, w=mm_tok, xw=x1_done.get(tq - 1, 0), tadd=tadd, tstat=tstat,
               tq=tq, cw=const_gld if tq == 0 else 0):
            if cw:
                v.wait_ge(s_cst.h, cw)
            v.wait_ge(s_mm.h, w)
            if xw:
                v.wait_ge(s_ve.h, xw)
            for dmb in range(4):
                ins = v.tensor_tensor(attmp[:, 512 * dmb:512 * (dmb + 1)],
                                      psO[tq % 2][:, 512 * dmb:512 * (dmb + 1)],
                                      bob_sb[:, 512 * dmb:512 * (dmb + 1)], ALU.add)
            ins.then_inc(s_ve.h, 1)
            v.wait_ge(s_ve.h, tadd)
            for g4 in range(4):
                ins = v.bn_stats(stat3[:, 6 * g4:6 * (g4 + 1)],
                                 attmp[:, 512 * g4:512 * (g4 + 1)])
            ins.then_inc(s_ve.h, 1)
            v.wait_ge(s_ve.h, tstat)
            v.bn_aggr(mv3[:], stat3[:]).then_inc(s_ve.h, 1)
        R["vector"].emit(fn)

        def fn(sc, w=ve_tok, lw=const_ld if tq == 0 else 0):
            if lw:
                sc.wait_ge(s_ld.h, lw)
            sc.wait_ge(s_ve.h, w)
            sc.activation(sd3[:], mv3[:, 1:2], AF.Sqrt, bias=cv_sb[:, 0:1]).then_inc(s_sc.h, 1)
        R["scalar"].emit(fn)
        sc_tok = s_sc.inc()

        tr3 = s_ve.inc(); tn3 = s_ve.inc()
        x1_done[tq] = s_ve.inc()

        def fn(v, w=sc_tok, bi=bi, xw=xw, sw=x1_sp.get(tq - 2, (0, 0)),
               tr3=tr3, tn3=tn3):
            v.wait_ge(s_sc.h, w)
            v.wait_ge(sx[bi].h, xw)
            if sw[0]:
                v.wait_ge(ssp[4 + bi].h, sw[0])
            if sw[1]:
                v.wait_ge(sxs[bi].h, sw[1])
            v.reciprocal(rstd3[:], sd3[:]).then_inc(s_ve.h, 1)
            v.wait_ge(s_ve.h, tr3)
            v.tensor_scalar(attmp[:], attmp[:], mv3[:, 0:1], rstd3[:],
                            ALU.subtract, ALU.mult).then_inc(s_ve.h, 1)
            v.wait_ge(s_ve.h, tn3)
            v.tensor_add(x1tl[bi][:], attmp[:], xrb[bi][:]).then_inc(s_ve.h, 1)
        R["vector"].emit(fn)

        def fn(s, w=x1_done[tq], tq=tq, bi=bi):
            s.wait_ge(s_ve.h, w)
            s.dma_start(out=out_h[128 * tq:128 * (tq + 1), :],
                        in_=x1tl[bi][:]).then_inc(ssp[4 + bi].h, 16)
        R["sync"].emit(fn)

        def fn(g, w=x1_done[tq], tq=tq, bi=bi):
            g.wait_ge(s_ve.h, w)
            g.dma_start(out=x1s[128 * tq:128 * (tq + 1), :],
                        in_=x1tl[bi][:]).then_inc(sxs[bi].h, 16)
        R["gpsimd"].emit(fn)
        ssp[4 + bi].inc(16)
        x1_sp[tq] = (ssp[4 + bi].n, sxs[bi].inc(16))

    barrier()
    p3.close()
    if STOP <= 4:
        cm.close()
        return _finish()

    p4a = contextlib.ExitStack()
    asb = p4a.enter_context
    w1t_sb = asb(nc.sbuf_tensor("w1t_sb", [128, 16 * D], BF16))
    wrt_sb = asb(nc.sbuf_tensor("wrt_sb", [128, 16 * 8], BF16))
    x1b = [asb(nc.sbuf_tensor(f"x1b{i}", [128, D], BF16)) for i in range(2)]
    y2b = [asb(nc.sbuf_tensor(f"y2b{i}", [128, D], BF16)) for i in range(2)]
    y2t = [asb(nc.sbuf_tensor(f"y2t{i}", [128, 16 * QW], BF16)) for i in range(2)]
    stat4 = asb(nc.sbuf_tensor("stat4", [128, 24], F32))
    mv4 = asb(nc.sbuf_tensor("mv4", [128, 2], F32))
    sd4 = asb(nc.sbuf_tensor("sd4", [128, 1], F32))
    rstd4 = asb(nc.sbuf_tensor("rstd4", [128, 1], F32))
    h1stg = [asb(nc.sbuf_tensor(f"h1stg{i}", [128, QW], BF16)) for i in range(2)]
    rew = asb(nc.sbuf_tensor("rew", [128, 8], F32))
    rm1 = asb(nc.sbuf_tensor("rm1", [128, 1], F32))
    rm2 = asb(nc.sbuf_tensor("rm2", [128, 1], F32))
    rd_ = asb(nc.sbuf_tensor("rd_", [128, 8], F32))
    rmask = asb(nc.sbuf_tensor("rmask", [128, 8], F32))
    rden = asb(nc.sbuf_tensor("rden", [128, 1], F32))
    rrec = asb(nc.sbuf_tensor("rrec", [128, 1], F32))
    psT4 = p4a.enter_context(nc.psum_tensor("psT4", [128, 1024], F32))
    psR = p4a.enter_context(nc.psum_tensor("psR", [128, 512], F32))
    psH = [p4a.enter_context(nc.psum_tensor(f"psH{i}", [128, 512], F32)) for i in range(2)]

    def gn(g, w4=ssp[4].n, w5=ssp[5].n, x4=sxs[0].n, x5=sxs[1].n):
        g.wait_ge(ssp[4].h, w4)
        g.wait_ge(ssp[5].h, w5)
        g.wait_ge(sxs[0].h, x4)
        g.wait_ge(sxs[1].h, x5)
    R["gpsimd"].emit(gn)
    load_weight_cast(w1t_sb, w1t_h, D, "w1t", s_w1)

    def gn(g):
        src = wrt_h[:, :].rearrange("(c p) o -> p c o", p=128)
        dst = wrt_sb[:].rearrange("p (c o) -> p c o", o=8)
        g.dma_start(out=dst, in_=src).then_inc(s_w1.h, 16)
    R["gpsimd"].emit(gn)
    w1_tok = s_w1.inc(16)

    y2t_free = [0, 0]
    ln2_done = {}
    tr_evac4 = {}
    ytr2_done = {}
    rexp_tok = {}
    rg_tok = {}
    h1_si = {}
    h1_sp = {"a": [0, 0]}
    for qt4 in range(4):
        y2i = qt4 % 2
        for s4 in range(4):
            tq = 4 * qt4 + s4
            bi = tq % 2

            def fn(s, tq=tq, bi=bi, w=ln2_done.get(tq - 2, 0),
                   w4=sxs[0].n if tq < 2 else 0, w5=sxs[1].n if tq < 2 else 0):
                if w4:
                    s.wait_ge(sxs[0].h, w4)
                    s.wait_ge(sxs[1].h, w5)
                if w:
                    s.wait_ge(s_ve.h, w)
                s.dma_start(out=x1b[bi][:],
                            in_=x1s[128 * tq:128 * (tq + 1), :]).then_inc(sx[bi].h, 16)
            R["sync"].emit(fn)
            xw = sx[bi].inc(16)

            bnst4 = s_ve.inc()
            bn_tok = s_ve.inc()

            def fn(v, w=xw, bi=bi, pw=ln2_done.get(tq - 1, 0), st=bnst4):
                v.wait_ge(sx[bi].h, w)
                if pw:
                    v.wait_ge(s_ve.h, pw)
                for g4 in range(4):
                    ins = v.bn_stats(stat4[:, 6 * g4:6 * (g4 + 1)],
                                     x1b[bi][:, 512 * g4:512 * (g4 + 1)])
                ins.then_inc(s_ve.h, 1)
                v.wait_ge(s_ve.h, st)
                v.bn_aggr(mv4[:], stat4[:]).then_inc(s_ve.h, 1)
            R["vector"].emit(fn)

            def fn(sc, w=bn_tok):
                sc.wait_ge(s_ve.h, w)
                sc.activation(sd4[:], mv4[:, 1:2], AF.Sqrt, bias=cv_sb[:, 0:1]).then_inc(s_sc.h, 1)
            R["scalar"].emit(fn)
            sq4_tok = s_sc.inc()

            rcp4 = s_ve.inc()
            ln2_done[tq] = s_ve.inc()

            def fn(v, w=sq4_tok, bi=bi, tw=ytr2_done.get(tq - 2, 0), rt=rcp4):
                v.wait_ge(s_sc.h, w)
                if tw:
                    v.wait_ge(s_mm.h, tw)
                v.reciprocal(rstd4[:], sd4[:]).then_inc(s_ve.h, 1)
                v.wait_ge(s_ve.h, rt)
                v.tensor_scalar(y2b[bi][:], x1b[bi][:], mv4[:, 0:1], rstd4[:],
                                ALU.subtract, ALU.mult).then_inc(s_ve.h, 1)
            R["vector"].emit(fn)

            for c in range(16):
                slot = c % 2

                def fn(tn, bi=bi, c=c, slot=slot, w=ln2_done[tq],
                       ew=tr_evac4.get((tq, c - 2), 0),
                       cw=const_gld if (tq == 0 and c == 0) else 0):
                    if cw:
                        tn.wait_ge(s_cst.h, cw)
                    if c == 0:
                        tn.wait_ge(s_ve.h, w)
                    if ew:
                        tn.wait_ge(s_ve.h, ew)
                    tn.matmul(psT4[:, 512 * slot:512 * slot + 128],
                              y2b[bi][:, 128 * c:128 * (c + 1)], eye_sb[:],
                              start=True, stop=True).then_inc(s_mm.h, 1)
                R["tensor"].emit(fn)
                mm_tok = s_mm.inc()

                def fn(v, w=mm_tok, c=c, slot=slot, s4=s4, y2i=y2i,
                       fw=y2t_free[y2i] if (s4 == 0 and c == 0) else 0):
                    v.wait_ge(s_mm.h, w)
                    if fw:
                        v.wait_ge(s_mm.h, fw)
                    v.tensor_copy(
                        y2t[y2i][:, 512 * c + 128 * s4:512 * c + 128 * (s4 + 1)],
                        psT4[:, 512 * slot:512 * slot + 128]).then_inc(s_ve.h, 1)
                R["vector"].emit(fn)
                tr_evac4[(tq, c)] = s_ve.inc()
            ytr2_done[tq] = s_mm.n
            tp_tok = tr_evac4[(tq, 15)]

            def fn(tn, w=tp_tok, wg=w1_tok, s4=s4, y2i=y2i, tq=tq,
                   pw=rexp_tok.get(tq - 1, 0)):
                tn.wait_ge(s_ve.h, w)
                if tq == 0:
                    tn.wait_ge(s_w1.h, wg)
                if pw:
                    tn.wait_ge(s_sc.h, pw)
                for dc in range(16):
                    ins = tn.matmul(psR[:, 0:8],
                              y2t[y2i][:, 512 * dc + 128 * s4:512 * dc + 128 * (s4 + 1)],
                              wrt_sb[:, 8 * dc:8 * (dc + 1)],
                              start=(dc == 0), stop=(dc == 15))
                ins.then_inc(s_mm.h, 1)
            R["tensor"].emit(fn)
            rmm = s_mm.inc()

            def fn(v, w=rmm, gw=rg_tok.get(tq - 1, 0)):
                v.wait_ge(s_mm.h, w)
                if gw:
                    v.wait_ge(s_ve.h, gw)
                v.reduce_max(rm1[:], psR[:, 0:8], axis=AX.X, negate=True
                             ).then_inc(s_ve.h, 1)
            R["vector"].emit(fn)
            nm_tok = s_ve.inc()

            def fn(sc, w=nm_tok):
                sc.wait_ge(s_ve.h, w)
                sc.activation(rew[:], psR[:, 0:8], AF.Exp, bias=rm1[:]
                              ).then_inc(s_sc.h, 1)
            R["scalar"].emit(fn)
            rexp_tok[tq] = s_sc.inc()

            rtoks = [s_ve.inc() for _ in range(10)]
            rg_tok[tq] = s_ve.inc()

            def fn(v, w=rexp_tok[tq], tq=tq, rtoks=rtoks):
                v.wait_ge(s_sc.h, w)
                v.reduce_max(rm1[:], rew[:], axis=AX.X).then_inc(s_ve.h, 1)
                v.wait_ge(s_ve.h, rtoks[0])
                v.tensor_scalar(rd_[:], rew[:], rm1[:], None,
                                ALU.subtract).then_inc(s_ve.h, 1)
                v.wait_ge(s_ve.h, rtoks[1])
                v.tensor_scalar(rmask[:], rd_[:], 0.0, None,
                                ALU.is_ge).then_inc(s_ve.h, 1)
                v.wait_ge(s_ve.h, rtoks[2])
                v.scalar_tensor_tensor(rd_[:], rmask[:], -1e30, rew[:],
                                       ALU.mult, ALU.add).then_inc(s_ve.h, 1)
                v.wait_ge(s_ve.h, rtoks[3])
                v.reduce_max(rm2[:], rd_[:], axis=AX.X).then_inc(s_ve.h, 1)
                v.wait_ge(s_ve.h, rtoks[4])
                v.tensor_scalar(rd_[:], rew[:], rm2[:], None,
                                ALU.subtract).then_inc(s_ve.h, 1)
                v.wait_ge(s_ve.h, rtoks[5])
                v.tensor_scalar(rmask[:], rd_[:], 0.0, None,
                                ALU.is_ge).then_inc(s_ve.h, 1)
                v.wait_ge(s_ve.h, rtoks[6])
                v.tensor_tensor(rd_[:], rew[:], rmask[:],
                                ALU.mult).then_inc(s_ve.h, 1)
                v.wait_ge(s_ve.h, rtoks[7])
                v.tensor_tensor(rden[:], rm1[:], rm2[:], ALU.add).then_inc(s_ve.h, 1)
                v.wait_ge(s_ve.h, rtoks[8])
                v.reciprocal(rrec[:], rden[:]).then_inc(s_ve.h, 1)
                v.wait_ge(s_ve.h, rtoks[9])
                v.tensor_scalar(gall[:, 8 * tq:8 * (tq + 1)], rd_[:], rrec[:],
                                None, ALU.mult).then_inc(s_ve.h, 1)
            R["vector"].emit(fn)

        tpw = tr_evac4[(4 * qt4 + 3, 15)]
        for ic in range(16):
            pi = ic % 2

            def fn(tn, w=tpw, ic=ic, pi=pi, y2i=y2i,
                   ew=h1_si.get((qt4, ic - 2), 0) or
                      (h1_si.get((qt4 - 1, 14 + (ic)), 0) if ic < 2 else 0)):
                if ic == 0:
                    tn.wait_ge(s_ve.h, w)
                if ew:
                    tn.wait_ge(s_sc.h, ew)
                for dc in range(16):
                    ins = tn.matmul(psH[pi][:],
                              w1t_sb[:, D * dc + 128 * ic:D * dc + 128 * (ic + 1)],
                              y2t[y2i][:, 512 * dc:512 * (dc + 1)],
                              start=(dc == 0), stop=(dc == 15))
                ins.then_inc(s_mm.h, 1)
            R["tensor"].emit(fn)
            mm_tok = s_mm.inc()

            def fn(sc, w=mm_tok, pi=pi, sw=h1_sp["a"][pi]):
                sc.wait_ge(s_mm.h, w)
                if sw:
                    sc.wait_ge(ssp[pi].h, sw)
                sc.activation(h1stg[pi][:], psH[pi][:], AF.Silu).then_inc(s_sc.h, 1)
            R["scalar"].emit(fn)
            h1_si[(qt4, ic)] = si_tok = s_sc.inc()

            def fn(s, w=si_tok, ic=ic, qt4=qt4, pi=pi):
                s.wait_ge(s_sc.h, w)
                s.dma_start(out=sh1[128 * ic:128 * (ic + 1), QW * qt4:QW * (qt4 + 1)],
                            in_=h1stg[pi][:]).then_inc(ssp[pi].h, 16)
            R["sync"].emit(fn)
            h1_sp["a"][pi] = ssp[pi].inc(16)

        y2t_free[y2i] = s_mm.n

    def fn(s, w0=ssp[0].n, w1=ssp[1].n):
        s.wait_ge(ssp[0].h, w0)
        s.wait_ge(ssp[1].h, w1)
    R["sync"].emit(fn)
    barrier()
    p4a.close()
    if STOP <= 5:
        cm.close()
        return _finish()

    p4b = contextlib.ExitStack()
    bsb = p4b.enter_context
    w3t_sb = bsb(nc.sbuf_tensor("w3t_sb", [128, 16 * D], BF16))
    h1str = [bsb(nc.sbuf_tensor(f"h1str{i}", [128, 16 * 128], BF16)) for i in range(2)]
    h2b = [bsb(nc.sbuf_tensor(f"h2b{i}", [128, D], BF16)) for i in range(2)]
    psH2 = [p4b.enter_context(nc.psum_tensor(f"psH2{i}", [128, 512], F32)) for i in range(2)]
    psSp = p4b.enter_context(nc.psum_tensor("psSp", [8, 2048], F32))

    def gn(g, w0=ssp[0].n, w1=ssp[1].n):
        g.wait_ge(ssp[0].h, w0)
        g.wait_ge(ssp[1].h, w1)
    R["gpsimd"].emit(gn)
    w3_tok = load_weight_cast(w3t_sb, w3t_h, D, "w3t", s_w3)

    sp_done = {}
    h2_si = {}
    for tq in range(16):
        bi = tq % 2

        def fn(s, tq=tq, bi=bi, w=sp_done.get(tq - 2, 0)):
            if w:
                s.wait_ge(s_mm.h, w)
            src = sh1[:, 128 * tq:128 * (tq + 1)].rearrange(
                "(c p) q -> p c q", p=128)
            dst = h1str[bi][:].rearrange("p (c q) -> p c q", q=128)
            s.dma_start(out=dst, in_=src).then_inc(sld[bi].h, 16)
        R["sync"].emit(fn)
        ld = sld[bi].inc(16)

        for jt in range(4):
            pi = jt % 2

            def fn(tn, w=ld, wg=w3_tok, jt=jt, pi=pi, bi=bi, tq=tq,
                   ew=h2_si.get((tq, jt - 2), 0) or
                      (h2_si.get((tq - 1, 2 + jt), 0) if jt < 2 else 0)):
                if jt == 0:
                    tn.wait_ge(sld[bi].h, w)
                    if tq == 0:
                        tn.wait_ge(s_w3.h, wg)
                if ew:
                    tn.wait_ge(s_sc.h, ew)
                for ic in range(16):
                    ins = tn.matmul(psH2[pi][:],
                              h1str[bi][:, 128 * ic:128 * (ic + 1)],
                              w3t_sb[:, D * ic + 512 * jt:D * ic + 512 * (jt + 1)],
                              start=(ic == 0), stop=(ic == 15))
                ins.then_inc(s_mm.h, 1)
            R["tensor"].emit(fn)
            mm_tok = s_mm.inc()

            def fn(sc, w=mm_tok, jt=jt, pi=pi, bi=bi, tq=tq,
                   hw=sp_done.get(tq - 2, 0)):
                sc.wait_ge(s_mm.h, w)
                if jt == 0 and hw:
                    sc.wait_ge(s_mm.h, hw)
                sc.activation(h2b[bi][:, 512 * jt:512 * (jt + 1)], psH2[pi][:],
                              AF.Silu).then_inc(s_sc.h, 1)
            R["scalar"].emit(fn)
            h2_si[(tq, jt)] = s_sc.inc()
        h2done = h2_si[(tq, 3)]

        def fn(tn, w=h2done, tq=tq, bi=bi, rgw=rg_tok[tq]):
            tn.wait_ge(s_sc.h, w)
            tn.wait_ge(s_ve.h, rgw)
            for jt in range(4):
                ins = tn.matmul(psSp[:, 512 * jt:512 * (jt + 1)],
                          gall[:, 8 * tq:8 * (tq + 1)],
                          h2b[bi][:, 512 * jt:512 * (jt + 1)],
                          start=(tq == 0), stop=(tq == 15))
            ins.then_inc(s_mm.h, 1)
        R["tensor"].emit(fn)
        sp_done[tq] = s_mm.inc()

    tspf = s_ve.inc()
    spre_tok = s_ve.inc()

    def fn(v, w=sp_done[15], tspf=tspf):
        v.wait_ge(s_mm.h, w)
        v.tensor_copy(spre_f[:], psSp[:]).then_inc(s_ve.h, 1)
        v.wait_ge(s_ve.h, tspf)
        v.tensor_copy(spre_b[:], spre_f[:]).then_inc(s_ve.h, 1)
    R["vector"].emit(fn)

    if mode == "l2":
        def fn(s, w=spre_tok):
            s.wait_ge(s_ve.h, w)
            s.dma_start(out=spre_h[:, :], in_=spre_f[:]).then_inc(s_st.h, 16)
        R["sync"].emit(fn)
        spre_st = s_st.inc(16)

        def fn(s, w=spre_st, w4=ssp[4].n, w5=ssp[5].n):
            s.wait_ge(ssp[4].h, w4)
            s.wait_ge(ssp[5].h, w5)
            s.wait_ge(s_st.h, w)
        R["sync"].emit(fn)
        p4b.close()
        cm.close()
        return _finish()

    barrier()
    p4b.close()
    if STOP <= 6:
        cm.close()
        return _finish()

    p4c = contextlib.ExitStack()
    csb = p4c.enter_context
    w2t_sb = csb(nc.sbuf_tensor("w2t_sb", [128, 16 * D], BF16))
    spreT = csb(nc.sbuf_tensor("spreT", [128, 16 * 8], BF16))
    ssum_st = csb(nc.sbuf_tensor("ssum_st", [8, D], F32))
    Ssb = csb(nc.sbuf_tensor("Ssb", [8, D], F32))
    statS = csb(nc.sbuf_tensor("statS", [8, 24], F32))
    mvS = csb(nc.sbuf_tensor("mvS", [8, 2], F32))
    sdS = csb(nc.sbuf_tensor("sdS", [8, 1], F32))
    rstdS = csb(nc.sbuf_tensor("rstdS", [8, 1], F32))
    x18 = csb(nc.sbuf_tensor("x18", [8, D], F32))
    fix8 = csb(nc.sbuf_tensor("fix8", [8, D], F32))
    psT5 = p4c.enter_context(nc.psum_tensor("psT5", [128, 512], F32))
    psS2 = p4c.enter_context(nc.psum_tensor("psS2", [8, 2048], F32))

    w2_tok = load_weight_cast(w2t_sb, w2t_h, D, "w2t", s_w2)

    t5_evac = {}
    for t16 in range(16):
        def fn(tn, w=spre_tok, t16=t16, ew=t5_evac.get(t16 - 1, 0)):
            if t16 == 0:
                tn.wait_ge(s_ve.h, w)
            if ew:
                tn.wait_ge(s_ve.h, ew)
            tn.matmul(psT5[:, 0:8],
                      spre_b[:, 128 * t16:128 * (t16 + 1)], eye_sb[0:8, 0:8],
                      start=True, stop=True).then_inc(s_mm.h, 1)
        R["tensor"].emit(fn)
        mm_tok = s_mm.inc()

        def fn(v, w=mm_tok, t16=t16):
            v.wait_ge(s_mm.h, w)
            v.tensor_copy(spreT[:, 8 * t16:8 * (t16 + 1)],
                          psT5[:, 0:8]).then_inc(s_ve.h, 1)
        R["vector"].emit(fn)
        t5_evac[t16] = s_ve.inc()

    def fn(tn, w=t5_evac[15], wg=w2_tok):
        tn.wait_ge(s_ve.h, w)
        tn.wait_ge(s_w2.h, wg)
        for dt in range(4):
            for ic in range(16):
                ins = tn.matmul(psS2[:, 512 * dt:512 * (dt + 1)],
                          spreT[:, 8 * ic:8 * (ic + 1)],
                          w2t_sb[:, D * ic + 512 * dt:D * ic + 512 * (dt + 1)],
                          start=(ic == 0), stop=(ic == 15))
        ins.then_inc(s_mm.h, 1)
    R["tensor"].emit(fn)
    smm_tok = s_mm.inc()

    def fn(v, w=smm_tok):
        v.wait_ge(s_mm.h, w)
        v.tensor_copy(ssum_st[:], psS2[:]).then_inc(s_ve.h, 1)
    R["vector"].emit(fn)
    sst_tok = s_ve.inc()

    def fn(s, w=sst_tok):
        s.wait_ge(s_ve.h, w)
        s.dma_start(out=ssum_in[:, :], in_=ssum_st[:]).then_inc(ssp[2].h, 16)
    R["sync"].emit(fn)
    ssin_tok = ssp[2].inc(16)

    def fn(g, w=ssin_tok):
        g.wait_ge(ssp[2].h, w)
        g.collective_compute(
            "AllReduce", ALU.add,
            replica_groups=[[0, 1, 2, 3, 4, 5, 6, 7]],
            ins=[ssum_in[:, :]],
            outs=[ssum_out[:, :]],
        ).then_inc(s_cc.h, 16)
    R["gpsimd"].emit(fn)
    ar_tok = s_cc.inc(16)

    def fn(s, w=ar_tok):
        s.wait_ge(s_cc.h, w)
        s.dma_start(out=Ssb[:], in_=ssum_out[:, :]).then_inc(scm.h, 16)
        s.dma_start(out=x18[:], in_=x1s[0:8, :]).then_inc(scm.h, 16)
    R["sync"].emit(fn)
    fin_ld = scm.inc(32)

    tstS = s_ve.inc()
    bnS_tok = s_ve.inc()

    def fn(v, w=fin_ld, tstS=tstS):
        v.wait_ge(scm.h, w)
        for g4 in range(4):
            ins = v.bn_stats(statS[:, 6 * g4:6 * (g4 + 1)],
                             Ssb[:, 512 * g4:512 * (g4 + 1)])
        ins.then_inc(s_ve.h, 1)
        v.wait_ge(s_ve.h, tstS)
        v.bn_aggr(mvS[:], statS[:]).then_inc(s_ve.h, 1)
    R["vector"].emit(fn)

    def fn(sc, w=bnS_tok):
        sc.wait_ge(s_ve.h, w)
        sc.activation(sdS[:], mvS[:, 1:2], AF.Sqrt,
                      bias=cv_sb[0:8, 0:1]).then_inc(s_sc.h, 1)
    R["scalar"].emit(fn)
    sqS_tok = s_sc.inc()

    tf = [s_ve.inc() for _ in range(3)]
    fix_tok = s_ve.inc()

    def fn(v, w=sqS_tok, tf=tf):
        v.wait_ge(s_sc.h, w)
        v.reciprocal(rstdS[:], sdS[:]).then_inc(s_ve.h, 1)
        v.wait_ge(s_ve.h, tf[0])
        v.tensor_scalar(Ssb[:], Ssb[:], mvS[:, 0:1], rstdS[:],
                        ALU.subtract, ALU.mult).then_inc(s_ve.h, 1)
        v.wait_ge(s_ve.h, tf[1])
        v.tensor_scalar(Ssb[:], Ssb[:], gate_sb[0:8, 0:1], None,
                        ALU.mult).then_inc(s_ve.h, 1)
        v.wait_ge(s_ve.h, tf[2])
        v.tensor_add(fix8[:], x18[:], Ssb[:]).then_inc(s_ve.h, 1)
    R["vector"].emit(fn)

    def fn(s, w=fix_tok):
        s.wait_ge(s_ve.h, w)
        s.dma_start(out=out_h[0:8, :], in_=fix8[:]).then_inc(s_st.h, 16)
    R["sync"].emit(fn)
    final_st = s_st.inc(16)

    def fn(s, w=final_st, w4=ssp[4].n, w5=ssp[5].n):
        s.wait_ge(ssp[4].h, w4)
        s.wait_ge(ssp[5].h, w5)
        s.wait_ge(s_st.h, w)
    R["sync"].emit(fn)

    p4c.close()
    cm.close()
    return _finish()


def build_nc3():
    nc = bass.Bass("TRN2", target_bir_lowering=False)
    spre_all = nc.dram_tensor("spre_all", [8, 8 * D], F32, kind="ExternalInput").ap()
    x18_h = nc.dram_tensor("x18", [8, D], F32, kind="ExternalInput").ap()
    w2t_h = nc.dram_tensor("w2t", [D, D], F32, kind="ExternalInput").ap()
    eye_h = nc.dram_tensor("eye", [128, 128], F32, kind="ExternalInput").ap()
    cvec_h = nc.dram_tensor("cvec", [128, 2], F32, kind="ExternalInput").ap()
    out8_h = nc.dram_tensor("out8", [8, D], F32, kind="ExternalOutput").ap()

    R = {k: Rec() for k in ENG}
    cm = contextlib.ExitStack()

    def _finish():
        with nc.Block() as block:
            @block.sync
            def _(e):
                R["sync"].run(e)

            @block.tensor
            def _(e):
                R["tensor"].run(e)

            @block.vector
            def _(e):
                R["vector"].run(e)

            @block.scalar
            def _(e):
                R["scalar"].run(e)

            @block.gpsimd
            def _(e):
                R["gpsimd"].run(e)
        return nc

    def mksem(name):
        return Sem(cm.enter_context(nc.semaphore(name)))

    s_ld = mksem("s_ld")
    s_gld = mksem("s_gld")
    s_mm = mksem("s_mm")
    s_ve = mksem("s_ve")
    s_sc = mksem("s_sc")
    s_st = mksem("s_st")
    sb = cm.enter_context

    sp_all = sb(nc.sbuf_tensor("sp_all", [8, 8 * D], F32))
    tmp4 = sb(nc.sbuf_tensor("tmp4", [8, 4 * D], F32))
    ssum = sb(nc.sbuf_tensor("ssum", [8, D], F32))
    ssum_b = sb(nc.sbuf_tensor("ssum_b", [8, D], BF16))
    spreT = sb(nc.sbuf_tensor("spreT", [128, 16 * 8], BF16))
    eye_sb = sb(nc.sbuf_tensor("eye_sb", [128, 128], BF16))
    cv_sb = sb(nc.sbuf_tensor("cv_sb", [128, 2], F32))
    w2t_sb = sb(nc.sbuf_tensor("w2t_sb", [128, 16 * D], BF16))
    Ssb = sb(nc.sbuf_tensor("Ssb", [8, D], F32))
    statS = sb(nc.sbuf_tensor("statS", [8, 24], F32))
    mvS = sb(nc.sbuf_tensor("mvS", [8, 2], F32))
    sdS = sb(nc.sbuf_tensor("sdS", [8, 1], F32))
    rstdS = sb(nc.sbuf_tensor("rstdS", [8, 1], F32))
    x18 = sb(nc.sbuf_tensor("x18_sb", [8, D], F32))
    fix8 = sb(nc.sbuf_tensor("fix8", [8, D], F32))
    psT5 = cm.enter_context(nc.psum_tensor("psT5", [128, 512], F32))
    psS2 = cm.enter_context(nc.psum_tensor("psS2", [8, 2048], F32))

    def fn(s):
        s.dma_start(out=sp_all[:], in_=spre_all[:, :]).then_inc(s_ld.h, 16)
        s.dma_start(out=x18[:], in_=x18_h[:, :]).then_inc(s_ld.h, 16)
        s.dma_start(out=cv_sb[:], in_=cvec_h[:, :]).then_inc(s_ld.h, 16)
    R["sync"].emit(fn)
    ld_tok = s_ld.inc(48)

    def fn(g):
        g.dma_start(out=eye_sb[:], in_=eye_h[:, :]).then_inc(s_gld.h, 16)
        for c4 in range(4):
            dst = w2t_sb[:, 4 * D * c4: 4 * D * (c4 + 1)].rearrange(
                "p (c o) -> p c o", o=D)
            srcw = w2t_h[512 * c4:512 * (c4 + 1), :].rearrange(
                "(c p) o -> p c o", p=128)
            g.dma_start(out=dst, in_=srcw).then_inc(s_gld.h, 16)
    R["gpsimd"].emit(fn)
    gld_tok = s_gld.inc(80)

    lv1 = s_ve.inc(); lv2 = s_ve.inc(); sum_tok = s_ve.inc()

    def fn(v, w=ld_tok, lv1=lv1, lv2=lv2):
        v.wait_ge(s_ld.h, w)
        for j in range(4):
            ins = v.tensor_add(tmp4[:, D * j:D * (j + 1)],
                               sp_all[:, D * (2 * j):D * (2 * j + 1)],
                               sp_all[:, D * (2 * j + 1):D * (2 * j + 2)])
        ins.then_inc(s_ve.h, 1)
        v.wait_ge(s_ve.h, lv1)
        for j in range(2):
            ins = v.tensor_add(tmp4[:, D * (2 * j):D * (2 * j + 1)] if False else
                               sp_all[:, D * j:D * (j + 1)],
                               tmp4[:, D * (2 * j):D * (2 * j + 1)],
                               tmp4[:, D * (2 * j + 1):D * (2 * j + 2)])
        ins.then_inc(s_ve.h, 1)
        v.wait_ge(s_ve.h, lv2)
        v.tensor_add(ssum[:], sp_all[:, 0:D], sp_all[:, D:2 * D]).then_inc(s_ve.h, 1)
        v.wait_ge(s_ve.h, sum_tok)
        v.tensor_copy(ssum_b[:], ssum[:]).then_inc(s_ve.h, 1)
    R["vector"].emit(fn)
    cp_tok = s_ve.inc()

    t5_evac = {}
    for t16 in range(16):
        def fn(tn, w=cp_tok, t16=t16, ew=t5_evac.get(t16 - 1, 0),
               gw=gld_tok if t16 == 0 else 0):
            if t16 == 0:
                tn.wait_ge(s_ve.h, w)
                tn.wait_ge(s_gld.h, gw)
            if ew:
                tn.wait_ge(s_ve.h, ew)
            tn.matmul(psT5[:, 0:8],
                      ssum_b[:, 128 * t16:128 * (t16 + 1)], eye_sb[0:8, 0:8],
                      start=True, stop=True).then_inc(s_mm.h, 1)
        R["tensor"].emit(fn)
        mm_tok = s_mm.inc()

        def fn(v, w=mm_tok, t16=t16):
            v.wait_ge(s_mm.h, w)
            v.tensor_copy(spreT[:, 8 * t16:8 * (t16 + 1)],
                          psT5[:, 0:8]).then_inc(s_ve.h, 1)
        R["vector"].emit(fn)
        t5_evac[t16] = s_ve.inc()

    def fn(tn, w=t5_evac[15]):
        tn.wait_ge(s_ve.h, w)
        for dt in range(4):
            for ic in range(16):
                ins = tn.matmul(psS2[:, 512 * dt:512 * (dt + 1)],
                          spreT[:, 8 * ic:8 * (ic + 1)],
                          w2t_sb[:, D * ic + 512 * dt:D * ic + 512 * (dt + 1)],
                          start=(ic == 0), stop=(ic == 15))
        ins.then_inc(s_mm.h, 1)
    R["tensor"].emit(fn)
    smm_tok = s_mm.inc()

    sc_tok0 = s_ve.inc()
    bnS_mid = s_ve.inc()
    bnS_tok = s_ve.inc()

    def fn(v, w=smm_tok, t0=sc_tok0, mid=bnS_mid):
        v.wait_ge(s_mm.h, w)
        v.tensor_copy(Ssb[:], psS2[:]).then_inc(s_ve.h, 1)
        v.wait_ge(s_ve.h, t0)
        for g4 in range(4):
            ins = v.bn_stats(statS[:, 6 * g4:6 * (g4 + 1)],
                             Ssb[:, 512 * g4:512 * (g4 + 1)])
        ins.then_inc(s_ve.h, 1)
        v.wait_ge(s_ve.h, mid)
        v.bn_aggr(mvS[:], statS[:]).then_inc(s_ve.h, 1)
    R["vector"].emit(fn)

    def fn(sc, w=bnS_tok):
        sc.wait_ge(s_ve.h, w)
        sc.activation(sdS[:], mvS[:, 1:2], AF.Sqrt,
                      bias=cv_sb[0:8, 0:1]).then_inc(s_sc.h, 1)
    R["scalar"].emit(fn)
    sqS_tok = s_sc.inc()

    tf0 = s_ve.inc(); tf1 = s_ve.inc()
    fix_tok = s_ve.inc()

    def fn(v, w=sqS_tok, tf0=tf0, tf1=tf1):
        v.wait_ge(s_sc.h, w)
        v.reciprocal(rstdS[:], sdS[:]).then_inc(s_ve.h, 1)
        v.wait_ge(s_ve.h, tf0)
        v.tensor_scalar(Ssb[:], Ssb[:], mvS[:, 0:1], rstdS[:],
                        ALU.subtract, ALU.mult).then_inc(s_ve.h, 1)
        v.wait_ge(s_ve.h, tf1)
        v.tensor_add(fix8[:], x18[:], Ssb[:]).then_inc(s_ve.h, 1)
    R["vector"].emit(fn)

    def fn(s, w=fix_tok):
        s.wait_ge(s_ve.h, w)
        s.dma_start(out=out8_h[:, :], in_=fix8[:]).then_inc(s_st.h, 16)
        s.wait_ge(s_st.h, 16)
    R["sync"].emit(fn)

    cm.close()
    return _finish()


def _host_inputs(inputs):
    x = np.asarray(inputs["L0_attn_x"], dtype=np.float32)
    Wq = np.asarray(inputs["L0_attn_Wq"], dtype=np.float32)
    bq = np.asarray(inputs["L0_attn_bq"], dtype=np.float32)
    Wk = np.asarray(inputs["L0_attn_Wk"], dtype=np.float32)
    bk = np.asarray(inputs["L0_attn_bk"], dtype=np.float32)
    Wv = np.asarray(inputs["L0_attn_Wv"], dtype=np.float32)
    bv = np.asarray(inputs["L0_attn_bv"], dtype=np.float32)
    Wo = np.asarray(inputs["L0_attn_Wo"], dtype=np.float32)
    bo = np.asarray(inputs["L0_attn_bo"], dtype=np.float32)
    w1 = np.asarray(inputs["L0_ffn_w1"], dtype=np.float32)
    w3 = np.asarray(inputs["L0_ffn_w3"], dtype=np.float32)
    w2 = np.asarray(inputs["L0_ffn_w2"], dtype=np.float32)
    wr = np.asarray(inputs["L0_ffn_wr"], dtype=np.float32)

    eye = np.eye(128, dtype=np.float32)
    ones128 = np.ones((128, 128), dtype=np.float32)
    masks = np.zeros((128, 4 * QW), dtype=np.float32)
    qs = np.arange(QW)
    ks = np.arange(128)
    for j in range(4):
        masks[:, QW * j:QW * (j + 1)] = (qs[None, :] >= 128 * j + ks[:, None])
    cvec = np.broadcast_to(
        np.array([EPS, -CSTAB], dtype=np.float32), (128, 2)).copy()

    w1t = np.ascontiguousarray(w1.T)
    w3t = np.ascontiguousarray(w3.T)
    w2t = np.ascontiguousarray(w2.T)
    wrt = np.ascontiguousarray(wr.T)

    maps1, maps2 = [], []
    for c in range(NCORE):
        b, h = c // 2, c % 2
        cols = slice(HALF * h, HALF * (h + 1))
        wo_r = np.ascontiguousarray(
            np.concatenate([Wo[HALF * h:HALF * (h + 1), :],
                            Wo[HALF * (1 - h):HALF * (2 - h), :]], axis=0))
        maps1.append({
            "xb": np.ascontiguousarray(x[b]),
            "xres": np.ascontiguousarray(x[b][TOK * h:TOK * (h + 1)]),
            "wq": np.ascontiguousarray(Wq),
            "wk": np.ascontiguousarray(Wk[:, cols]),
            "wv": np.ascontiguousarray(Wv[:, cols]),
            "bq": np.ascontiguousarray(bq.reshape(16, 128).T),
            "bk": np.ascontiguousarray(bk[cols].reshape(HPC, 128).T),
            "bv": np.ascontiguousarray(bv[cols].reshape(HPC, 128).T),
            "masks": masks,
            "ones128": ones128,
            "eye": eye,
            "cvec": cvec,
        })
        maps2.append({
            "xres": np.ascontiguousarray(x[b][TOK * h:TOK * (h + 1)]),
            "wo": wo_r,
            "bob": np.ascontiguousarray(np.broadcast_to(bo, (128, D))),
            "w1t": w1t,
            "w3t": w3t,
            "wrt": wrt,
            "eye": eye,
            "cvec": cvec,
        })
    return maps1, maps2, w2t, eye, cvec


def kernel(**inputs):
    maps1, maps2, w2t, eye, cvec = _host_inputs(inputs)

    nc1 = build_nc("l1")
    res1 = run_bass_kernel_spmd(nc1, maps1, core_ids=list(range(NCORE)))

    for c in range(NCORE):
        b, h = c // 2, c % 2
        partner = 2 * b + (1 - h)
        own = res1.results[c]["cfull"]
        other = res1.results[partner]["cfull"]
        maps2[c]["cm_h"] = np.ascontiguousarray(own[:, TOK * h:TOK * (h + 1)])
        maps2[c]["cp_h"] = np.ascontiguousarray(other[:, TOK * h:TOK * (h + 1)])

    nc2 = build_nc("l2")
    res2 = run_bass_kernel_spmd(nc2, maps2, core_ids=list(range(NCORE)))

    spre_all = np.stack([np.asarray(res2.results[c]["spre"], dtype=np.float32)
                         for c in range(NCORE)], axis=1).reshape(8, 8 * D)
    x18 = np.ascontiguousarray(
        np.asarray(res2.results[0]["out"], dtype=np.float32)[0:8, :])

    nc3 = build_nc3()
    res3 = run_bass_kernel_spmd(
        nc3, [{"spre_all": np.ascontiguousarray(spre_all), "x18": x18,
               "w2t": w2t, "eye": eye, "cvec": cvec}], core_ids=[0])

    out = np.empty((B, T, D), dtype=np.float32)
    for c in range(NCORE):
        b, h = c // 2, c % 2
        out[b, TOK * h:TOK * (h + 1), :] = res2.results[c]["out"]
    out[0, 0:8, :] = res3.results[0]["out8"]
    return out


# version 14 (final)
# speedup vs baseline: 1.0103x; 1.0103x over previous
import math
import contextlib
import numpy as np
import concourse.bass as bass
import concourse.mybir as mybir
from concourse.bass_utils import run_bass_kernel_spmd

F32 = mybir.dt.float32
BF16 = mybir.dt.bfloat16
AF = mybir.ActivationFunctionType
ALU = mybir.AluOpType
AX = mybir.AxisListType

B, T, D = 4, 4096, 2048
NH, DH = 16, 128
HPC = 8
HALF = 1024
TOK = 2048
NQT = 8
QW = 512
SCALE = 1.0 / math.sqrt(DH)
CSTAB = 40.0
EPS = 1e-5
NCORE = 8
ENG = ["sync", "tensor", "vector", "scalar", "gpsimd"]


class Sem:
    def __init__(self, handle):
        self.h = handle
        self.n = 0

    def inc(self, k=1):
        self.n += k
        return self.n


class Rec:
    def __init__(self):
        self.ops = []

    def emit(self, fn):
        self.ops.append(fn)

    def run(self, eng):
        for fn in self.ops:
            fn(eng)


import os
STOP = int(os.environ.get("ATH_STOP", "99"))


def build_nc(mode="fused"):
    nc = bass.Bass("TRN2", target_bir_lowering=False)

    P12 = mode in ("fused", "l1")
    P3P = mode in ("fused", "l2")
    if P12:
        xb = nc.dram_tensor("xb", [T, D], F32, kind="ExternalInput").ap()
        wq_h = nc.dram_tensor("wq", [D, D], F32, kind="ExternalInput").ap()
        wk_h = nc.dram_tensor("wk", [D, HALF], F32, kind="ExternalInput").ap()
        wv_h = nc.dram_tensor("wv", [D, HALF], F32, kind="ExternalInput").ap()
        bq_h = nc.dram_tensor("bq", [128, 16], F32, kind="ExternalInput").ap()
        bk_h = nc.dram_tensor("bk", [128, HPC], F32, kind="ExternalInput").ap()
        bv_h = nc.dram_tensor("bv", [128, HPC], F32, kind="ExternalInput").ap()
        masks_h = nc.dram_tensor("masks", [128, 4 * QW], F32, kind="ExternalInput").ap()
        ones_h = nc.dram_tensor("ones128", [128, 128], F32, kind="ExternalInput").ap()
    xres = nc.dram_tensor("xres", [TOK, D], F32, kind="ExternalInput").ap()
    eye_h = nc.dram_tensor("eye", [128, 128], F32, kind="ExternalInput").ap()
    cvec_h = nc.dram_tensor("cvec", [128, 2], F32, kind="ExternalInput").ap()
    if P3P:
        wo_h = nc.dram_tensor("wo", [D, D], F32, kind="ExternalInput").ap()
        bob_h = nc.dram_tensor("bob", [128, D], F32, kind="ExternalInput").ap()
        w1t_h = nc.dram_tensor("w1t", [D, D], F32, kind="ExternalInput").ap()
        w3t_h = nc.dram_tensor("w3t", [D, D], F32, kind="ExternalInput").ap()
        wrt_h = nc.dram_tensor("wrt", [D, 8], F32, kind="ExternalInput").ap()
        out_h = nc.dram_tensor("out", [TOK, D], F32, kind="ExternalOutput").ap()
    if mode == "fused":
        w2t_h = nc.dram_tensor("w2t", [D, D], F32, kind="ExternalInput").ap()
        psel_h = nc.dram_tensor("psel", [128, 1], F32, kind="ExternalInput").ap()
        npsel_h = nc.dram_tensor("npsel", [128, 1], F32, kind="ExternalInput").ap()
        gate_h = nc.dram_tensor("gate", [128, 1], F32, kind="ExternalInput").ap()
    if mode == "l2":
        cm_h = nc.dram_tensor("cm_h", [HALF, TOK], BF16, kind="ExternalInput").ap()
        cp_h = nc.dram_tensor("cp_h", [HALF, TOK], BF16, kind="ExternalInput").ap()
        spre_h = nc.dram_tensor("spre", [8, D], F32, kind="ExternalOutput").ap()

    if P12:
        sq = nc.dram_tensor("sq", [D, TOK], BF16, kind="Internal").ap()
        sk = nc.dram_tensor("sk", [HALF, T], BF16, kind="Internal").ap()
        sv = nc.dram_tensor("sv", [HALF, T], BF16, kind="Internal").ap()
        if mode == "l1":
            cfull = nc.dram_tensor("cfull", [HALF, T], BF16, kind="ExternalOutput").ap()
        else:
            cfull = nc.dram_tensor("cfull", [HALF, T], BF16, kind="Internal").ap()
    if mode == "fused":
        csend = nc.dram_tensor("csend", [HALF, TOK], BF16, kind="Internal").ap()
        cmine_d = nc.dram_tensor("cmine_d", [HALF, TOK], BF16, kind="Internal").ap()
        agout = nc.dram_tensor("agout", [2 * HALF, TOK], BF16, kind="Internal").ap()
        ssum_in = nc.dram_tensor("ssum_in", [8, D], F32, kind="Internal").ap()
        ssum_out = nc.dram_tensor("ssum_out", [8, D], F32, kind="Internal").ap()
    if P3P:
        x1s = nc.dram_tensor("x1s", [TOK, D], BF16, kind="Internal").ap()
        sh1 = nc.dram_tensor("sh1", [D, TOK], BF16, kind="Internal").ap()

    R = {k: Rec() for k in ENG}
    cm = contextlib.ExitStack()

    def _finish():
        with nc.Block() as block:
            @block.sync
            def _(e):
                R["sync"].run(e)

            @block.tensor
            def _(e):
                R["tensor"].run(e)

            @block.vector
            def _(e):
                R["vector"].run(e)

            @block.scalar
            def _(e):
                R["scalar"].run(e)

            @block.gpsimd
            def _(e):
                R["gpsimd"].run(e)
        return nc


    def mksem(name):
        return Sem(cm.enter_context(nc.semaphore(name)))

    s_ld = mksem("s_ld")
    s_gld = mksem("s_gld")
    s_mm = mksem("s_mm")
    s_ve = mksem("s_ve")
    s_sc = mksem("s_sc")
    s_cc = mksem("s_cc")
    s_cst = mksem("s_cst")
    s_wo = mksem("s_wo")
    s_w1 = mksem("s_w1")
    s_w3 = mksem("s_w3")
    s_w2 = mksem("s_w2")
    s_st = mksem("s_st")
    s_ph = mksem("s_ph")
    sx = [mksem(f"sx{i}") for i in range(2)]
    sxs = [mksem(f"sxs{i}") for i in range(2)]
    sld = [mksem(f"sld{i}") for i in range(2)]
    scm = mksem("scm")
    sag = mksem("sag")
    ssp = [mksem(f"ssp{i}") for i in range(6)]

    def barrier():
        for e in ENG:
            R[e].emit(lambda g: g.nop().then_inc(s_ph.h, 1))
        tgt = s_ph.inc(len(ENG))
        for e in ENG:
            R[e].emit(lambda g, t=tgt: g.wait_ge(s_ph.h, t))

    sb = cm.enter_context

    eye_sb = sb(nc.sbuf_tensor("eye_sb", [128, 128], BF16))
    bq_sb = sb(nc.sbuf_tensor("bq_sb", [128, 16], F32))
    bk_sb = sb(nc.sbuf_tensor("bk_sb", [128, HPC], F32))
    bv_sb = sb(nc.sbuf_tensor("bv_sb", [128, HPC], F32))
    bob_sb = sb(nc.sbuf_tensor("bob_sb", [128, D], BF16))
    psel_sb = sb(nc.sbuf_tensor("psel_sb", [128, 1], F32))
    npsel_sb = sb(nc.sbuf_tensor("npsel_sb", [128, 1], F32))
    gate_sb = sb(nc.sbuf_tensor("gate_sb", [128, 1], F32))
    cv_sb = sb(nc.sbuf_tensor("cv_sb", [128, 2], F32))

    _cl = [(cv_sb, cvec_h)]
    if P12:
        _cl += [(bq_sb, bq_h), (bk_sb, bk_h), (bv_sb, bv_h)]
    if mode == "fused":
        _cl += [(psel_sb, psel_h), (npsel_sb, npsel_h), (gate_sb, gate_h)]

    def fn(s, _cl=_cl):
        for dst, src in _cl:
            s.dma_start(out=dst[:], in_=src[:, :]).then_inc(s_ld.h, 16)
    R["sync"].emit(fn)
    const_ld = s_ld.inc(len(_cl) * 16)

    _gl = [(eye_sb, eye_h)]
    if P3P:
        _gl += [(bob_sb, bob_h)]

    def fn(g, _gl=_gl):
        for dst, src in _gl:
            g.dma_start(out=dst[:], in_=src[:, :]).then_inc(s_cst.h, 16)
    R["gpsimd"].emit(fn)
    const_gld = s_cst.inc(len(_gl) * 16)

    def load_weight_cast(wsb, wh, ncols, tag, sem=None):
        if sem is None:
            sem = s_gld
        for c4 in range(4):
            def gn(g, wsb=wsb, wh=wh, c4=c4, ncols=ncols, sem=sem):
                dst = wsb[:, 4 * ncols * c4: 4 * ncols * (c4 + 1)].rearrange(
                    "p (c o) -> p c o", o=ncols)
                src = wh[512 * c4:512 * (c4 + 1), :].rearrange(
                    "(c p) o -> p c o", p=128)
                g.dma_start(out=dst, in_=src).then_inc(sem.h, 16)
            R["gpsimd"].emit(gn)
            sem.inc(16)
        return sem.n

    if not P12:
        pass
    else:
        p1 = contextlib.ExitStack()
        p1sb = p1.enter_context
        wq_sb = p1sb(nc.sbuf_tensor("wq_sb", [128, 16 * D], BF16))
        wk_sb = p1sb(nc.sbuf_tensor("wk_sb", [128, 16 * HALF], BF16))
        wv_sb = p1sb(nc.sbuf_tensor("wv_sb", [128, 16 * HALF], BF16))
        xbuf = [p1sb(nc.sbuf_tensor(f"xbuf{i}", [128, D], F32)) for i in range(2)]
        ybuf = [p1sb(nc.sbuf_tensor(f"ybuf{i}", [128, D], BF16)) for i in range(2)]
        stat = [p1sb(nc.sbuf_tensor(f"stat_{i}", [128, 24], F32)) for i in range(2)]
        mv = [p1sb(nc.sbuf_tensor(f"mv_{i}", [128, 2], F32)) for i in range(2)]
        sd = [p1sb(nc.sbuf_tensor(f"sd_{i}", [128, 1], F32)) for i in range(2)]
        rstd = [p1sb(nc.sbuf_tensor(f"rstd_{i}", [128, 1], F32)) for i in range(2)]
        y1t = [p1sb(nc.sbuf_tensor(f"y1t{i}", [128, 16 * QW], BF16)) for i in range(2)]
        qst = [p1sb(nc.sbuf_tensor(f"qst{i}", [128, QW], BF16)) for i in range(2)]
        kst = [p1sb(nc.sbuf_tensor(f"kst{i}", [128, QW], BF16)) for i in range(2)]
        vst = [p1sb(nc.sbuf_tensor(f"vst{i}", [128, QW], BF16)) for i in range(2)]

        psT = p1.enter_context(nc.psum_tensor("psT", [128, 1024], F32))
        psQ = [p1.enter_context(nc.psum_tensor(f"psQ{i}", [128, 512], F32)) for i in range(2)]
        psK = [p1.enter_context(nc.psum_tensor(f"psK{i}", [128, 512], F32)) for i in range(2)]
        psV = [p1.enter_context(nc.psum_tensor(f"psV{i}", [128, 512], F32)) for i in range(2)]

        load_weight_cast(wq_sb, wq_h, D, "wq")
        load_weight_cast(wk_sb, wk_h, HALF, "wk")
        wld_tok = load_weight_cast(wv_sb, wv_h, HALF, "wv")

        ln_done = {}
        ytr_done = {}
        tp_evac = {}
        y1t_free = [0, 0]
        stg_tok = {"q": [0, 0], "k": [0, 0], "v": [0, 0]}
        ps_evac = {"q": [0, 0], "k": [0, 0], "v": [0, 0]}

        for qt in range(NQT):
            y1 = qt % 2
            for s4 in range(4):
                t = 4 * qt + s4
                bi = t % 2

                def fn(s, t=t, bi=bi, w=ln_done.get(4 * qt + s4 - 2, 0)):
                    if w:
                        s.wait_ge(s_ve.h, w)
                    s.dma_start(out=xbuf[bi][:], in_=xb[128 * t:128 * (t + 1), :]
                                ).then_inc(sx[bi].h, 16)
                R["sync"].emit(fn)
                xw = sx[bi].inc(16)

                bnst_tok = s_ve.inc()
                bn_tok = s_ve.inc()

                def fn(v, t=t, bi=bi, xw=xw, lw=const_ld, st=bnst_tok):
                    v.wait_ge(sx[bi].h, xw)
                    if t == 0:
                        v.wait_ge(s_ld.h, lw)
                    for g4 in range(4):
                        ins = v.bn_stats(stat[bi][:, 6 * g4:6 * (g4 + 1)],
                                         xbuf[bi][:, 512 * g4:512 * (g4 + 1)])
                    ins.then_inc(s_ve.h, 1)
                    v.wait_ge(s_ve.h, st)
                    v.bn_aggr(mv[bi][:], stat[bi][:]).then_inc(s_ve.h, 1)
                R["vector"].emit(fn)

                def fn(sc, w=bn_tok, bi=bi):
                    sc.wait_ge(s_ve.h, w)
                    sc.activation(sd[bi][:], mv[bi][:, 1:2], AF.Sqrt, bias=cv_sb[:, 0:1]).then_inc(s_sc.h, 1)
                R["scalar"].emit(fn)
                sq_tok = s_sc.inc()

                rcp_tok = s_ve.inc()
                ln_done[t] = s_ve.inc()

                def fn(v, w=sq_tok, bi=bi, tw=ytr_done.get(t - 2, 0), rt=rcp_tok):
                    v.wait_ge(s_sc.h, w)
                    if tw:
                        v.wait_ge(s_mm.h, tw)
                    v.reciprocal(rstd[bi][:], sd[bi][:]).then_inc(s_ve.h, 1)
                    v.wait_ge(s_ve.h, rt)
                    v.tensor_scalar(ybuf[bi][:], xbuf[bi][:], mv[bi][:, 0:1], rstd[bi][:],
                                    ALU.subtract, ALU.mult).then_inc(s_ve.h, 1)
                R["vector"].emit(fn)

                for c in range(16):
                    slot = c % 2

                    def fn(tn, bi=bi, c=c, slot=slot, w=ln_done[t],
                           gw=const_gld if t == 0 and c == 0 else 0,
                           ew=tp_evac.get((t, c - 2), 0)):
                        if c == 0:
                            tn.wait_ge(s_ve.h, w)
                            if gw:
                                tn.wait_ge(s_cst.h, gw)
                        if ew:
                            tn.wait_ge(s_ve.h, ew)
                        tn.matmul(psT[:, 512 * slot:512 * slot + 128],
                                  ybuf[bi][:, 128 * c:128 * (c + 1)], eye_sb[:],
                                  start=True, stop=True).then_inc(s_mm.h, 1)
                    R["tensor"].emit(fn)
                    mm_tok = s_mm.inc()

                    def fn(v, w=mm_tok, c=c, slot=slot, s4=s4, y1=y1,
                           fw=y1t_free[y1] if (s4 == 0 and c == 0) else 0):
                        v.wait_ge(s_mm.h, w)
                        if fw:
                            v.wait_ge(s_mm.h, fw)
                        v.tensor_copy(
                            y1t[y1][:, 512 * c + 128 * s4:512 * c + 128 * (s4 + 1)],
                            psT[:, 512 * slot:512 * slot + 128]).then_inc(s_ve.h, 1)
                    R["vector"].emit(fn)
                    tp_evac[(t, c)] = s_ve.inc()
                ytr_done[t] = s_mm.n

            tpw = tp_evac[(4 * qt + 3, 15)]

            for kind, kbase, wsb, bias_sb, scale2, dst, pst, stg in (
                    ("k", 2, wk_sb, bk_sb, None, sk, psK, kst),
                    ("v", 4, wv_sb, bv_sb, None, sv, psV, vst)):
                for oc in range(HPC):
                    pi = oc % 2

                    def fn(tn, w=tpw, wg=wld_tok, oc=oc, pi=pi, wsb=wsb, y1=y1,
                           pst=pst, ew=ps_evac[kind][pi]):
                        if oc < 2:
                            tn.wait_ge(s_ve.h, w)
                            tn.wait_ge(s_gld.h, wg)
                        if ew:
                            tn.wait_ge(s_ve.h, ew)
                        for dc in range(16):
                            ins = tn.matmul(pst[pi][:],
                                      wsb[:, HALF * dc + 128 * oc:HALF * dc + 128 * (oc + 1)],
                                      y1t[y1][:, 512 * dc:512 * (dc + 1)],
                                      start=(dc == 0), stop=(dc == 15))
                        ins.then_inc(s_mm.h, 1)
                    R["tensor"].emit(fn)
                    mm_tok = s_mm.inc()

                    def fn(v, w=mm_tok, oc=oc, pi=pi, bias_sb=bias_sb, scale2=scale2,
                           pst=pst, stg=stg, sw=stg_tok[kind][pi], kb2=kbase):
                        v.wait_ge(s_mm.h, w)
                        if sw:
                            v.wait_ge(ssp[kb2 + pi].h, sw)
                        if scale2 is not None:
                            v.tensor_scalar(stg[pi][:], pst[pi][:],
                                            bias_sb[:, oc:oc + 1], scale2,
                                            ALU.add, ALU.mult).then_inc(s_ve.h, 1)
                        else:
                            v.tensor_scalar(stg[pi][:], pst[pi][:],
                                            bias_sb[:, oc:oc + 1], None,
                                            ALU.add).then_inc(s_ve.h, 1)
                    R["vector"].emit(fn)
                    ps_evac[kind][pi] = ev_tok = s_ve.inc()

                    def fn(s, w=ev_tok, oc=oc, pi=pi, qt=qt, dst=dst, stg=stg,
                           kb2=kbase):
                        s.wait_ge(s_ve.h, w)
                        s.dma_start(out=dst[128 * oc:128 * (oc + 1), QW * qt:QW * (qt + 1)],
                                    in_=stg[pi][:]).then_inc(ssp[kb2 + pi].h, 16)
                    R["scalar"].emit(fn)
                    stg_tok[kind][pi] = ssp[kbase + pi].inc(16)

            y1t_free[y1] = s_mm.n

        for qq in range(4):
            yq = qq % 2
            for s4 in range(4):
                t = 32 + 4 * qq + s4
                bi = t % 2
                tl = 4 * qq + s4

                def fn(s, tl=tl, bi=bi, w=ln_done.get(32 + 4 * qq + s4 - 2, 0)):
                    if w:
                        s.wait_ge(s_ve.h, w)
                    s.dma_start(out=xbuf[bi][:], in_=xres[128 * tl:128 * (tl + 1), :]
                                ).then_inc(sx[bi].h, 16)
                R["sync"].emit(fn)
                xw = sx[bi].inc(16)

                bnst_tok = s_ve.inc()
                bn_tok = s_ve.inc()

                def fn(v, bi=bi, xw=xw, st=bnst_tok):
                    v.wait_ge(sx[bi].h, xw)
                    for g4 in range(4):
                        ins = v.bn_stats(stat[bi][:, 6 * g4:6 * (g4 + 1)],
                                         xbuf[bi][:, 512 * g4:512 * (g4 + 1)])
                    ins.then_inc(s_ve.h, 1)
                    v.wait_ge(s_ve.h, st)
                    v.bn_aggr(mv[bi][:], stat[bi][:]).then_inc(s_ve.h, 1)
                R["vector"].emit(fn)

                def fn(sc, w=bn_tok, bi=bi):
                    sc.wait_ge(s_ve.h, w)
                    sc.activation(sd[bi][:], mv[bi][:, 1:2], AF.Sqrt, bias=cv_sb[:, 0:1]).then_inc(s_sc.h, 1)
                R["scalar"].emit(fn)
                sq_tok = s_sc.inc()

                rcp_tok = s_ve.inc()
                ln_done[t] = s_ve.inc()

                def fn(v, w=sq_tok, bi=bi, tw=ytr_done.get(t - 2, 0), rt=rcp_tok):
                    v.wait_ge(s_sc.h, w)
                    if tw:
                        v.wait_ge(s_mm.h, tw)
                    v.reciprocal(rstd[bi][:], sd[bi][:]).then_inc(s_ve.h, 1)
                    v.wait_ge(s_ve.h, rt)
                    v.tensor_scalar(ybuf[bi][:], xbuf[bi][:], mv[bi][:, 0:1], rstd[bi][:],
                                    ALU.subtract, ALU.mult).then_inc(s_ve.h, 1)
                R["vector"].emit(fn)

                for c in range(16):
                    slot = c % 2

                    def fn(tn, bi=bi, c=c, slot=slot, w=ln_done[t],
                           ew=tp_evac.get((t, c - 2), 0)):
                        if c == 0:
                            tn.wait_ge(s_ve.h, w)
                        if ew:
                            tn.wait_ge(s_ve.h, ew)
                        tn.matmul(psT[:, 512 * slot:512 * slot + 128],
                                  ybuf[bi][:, 128 * c:128 * (c + 1)], eye_sb[:],
                                  start=True, stop=True).then_inc(s_mm.h, 1)
                    R["tensor"].emit(fn)
                    mm_tok = s_mm.inc()

                    def fn(v, w=mm_tok, c=c, slot=slot, s4=s4, yq=yq,
                           fw=y1t_free[yq] if (s4 == 0 and c == 0) else 0):
                        v.wait_ge(s_mm.h, w)
                        if fw:
                            v.wait_ge(s_mm.h, fw)
                        v.tensor_copy(
                            y1t[yq][:, 512 * c + 128 * s4:512 * c + 128 * (s4 + 1)],
                            psT[:, 512 * slot:512 * slot + 128]).then_inc(s_ve.h, 1)
                    R["vector"].emit(fn)
                    tp_evac[(t, c)] = s_ve.inc()
                ytr_done[t] = s_mm.n

            tpwq = tp_evac[(32 + 4 * qq + 3, 15)]
            for oc in range(16):
                pi = oc % 2

                def fn(tn, w=tpwq, oc=oc, pi=pi, yq=yq,
                       ew=ps_evac["q"][pi]):
                    if oc < 2:
                        tn.wait_ge(s_ve.h, w)
                    if ew:
                        tn.wait_ge(s_ve.h, ew)
                    for dc in range(16):
                        ins = tn.matmul(psQ[pi][:],
                                  wq_sb[:, D * dc + 128 * oc:D * dc + 128 * (oc + 1)],
                                  y1t[yq][:, 512 * dc:512 * (dc + 1)],
                                  start=(dc == 0), stop=(dc == 15))
                    ins.then_inc(s_mm.h, 1)
                R["tensor"].emit(fn)
                mm_tok = s_mm.inc()

                def fn(v, w=mm_tok, oc=oc, pi=pi, sw=stg_tok["q"][pi]):
                    v.wait_ge(s_mm.h, w)
                    if sw:
                        v.wait_ge(ssp[pi].h, sw)
                    v.tensor_scalar(qst[pi][:], psQ[pi][:],
                                    bq_sb[:, oc:oc + 1], SCALE,
                                    ALU.add, ALU.mult).then_inc(s_ve.h, 1)
                R["vector"].emit(fn)
                ps_evac["q"][pi] = ev_tok = s_ve.inc()

                def fn(s, w=ev_tok, oc=oc, pi=pi, qq=qq):
                    s.wait_ge(s_ve.h, w)
                    s.dma_start(out=sq[128 * oc:128 * (oc + 1), QW * qq:QW * (qq + 1)],
                                in_=qst[pi][:]).then_inc(ssp[pi].h, 16)
                R["scalar"].emit(fn)
                stg_tok["q"][pi] = ssp[pi].inc(16)

            y1t_free[yq] = s_mm.n

        def fn(s, ws=[(ssp[i].h, ssp[i].n) for i in range(6)]):
            for h, w in ws:
                if w:
                    s.wait_ge(h, w)
        R["sync"].emit(fn)
        barrier()
        p1.close()
        if STOP <= 1:
            cm.close()
            return _finish()

        p2 = contextlib.ExitStack()
        p2sb = p2.enter_context
        masks_sb = p2sb(nc.sbuf_tensor("masks_sb", [128, 4 * QW], BF16))
        ones_sb = p2sb(nc.sbuf_tensor("ones_sb", [128, 128], BF16))
        kstrip = [p2sb(nc.sbuf_tensor(f"kstrip{i}", [128, T], BF16)) for i in range(2)]
        vstrip = [p2sb(nc.sbuf_tensor(f"vstrip{i}", [128, T], BF16)) for i in range(2)]
        vstr_dh = [p2sb(nc.sbuf_tensor(f"vstr_dh{i}", [128, T], BF16)) for i in range(2)]
        qtile = [p2sb(nc.sbuf_tensor(f"qtile{i}", [128, QW], BF16)) for i in range(2)]
        eacc = [p2sb(nc.sbuf_tensor(f"eacc{i}", [128, QW], F32)) for i in range(2)]
        eaccb = [p2sb(nc.sbuf_tensor(f"eaccb{i}", [128, QW], BF16)) for i in range(2)]
        qslab = [p2sb(nc.sbuf_tensor(f"qslab{i}", [128, 16 * 256], BF16)) for i in range(2)]
        ebuf = [p2sb(nc.sbuf_tensor(f"ebuf{i}", [128, 32 * QW], BF16)) for i in range(2)]
        lnl = p2sb(nc.sbuf_tensor("lnl", [128, QW], F32))
        rwide = p2sb(nc.sbuf_tensor("rwide", [128, QW], F32))
        ctxstg = [p2sb(nc.sbuf_tensor(f"ctxstg{i}", [128, QW], BF16)) for i in range(2)]

        psS = p2.enter_context(nc.psum_tensor("psS", [128, 2048], F32))
        psL = p2.enter_context(nc.psum_tensor("psL", [128, 512], F32))
        psC = p2.enter_context(nc.psum_tensor("psC", [128, 512], F32))
        psT2 = p2.enter_context(nc.psum_tensor("psT2", [128, 1024], F32))

        def fn(g):
            g.dma_start(out=ones_sb[:], in_=ones_h[:, :]).then_inc(s_cst.h, 16)
            g.dma_start(out=masks_sb[:], in_=masks_h[:, :]).then_inc(s_cst.h, 16)
        R["gpsimd"].emit(fn)
        cst2 = s_cst.inc(32)

        it = 0
        ld_tok = {}
        vtr_evac = {}
        vtr_done = {}
        exp_tok = {}
        pv_tok = {}
        r_tok = {}
        ctx_tok = {}
        ctx_sp = {}
        qcp_tok = {}
        sqs = [mksem(f"sqs{i}") for i in range(2)]
        sks = [mksem(f"sks{i}") for i in range(2)]
        svs = [mksem(f"svs{i}") for i in range(2)]

        qs_toks, ks_toks, vs_toks, vtr_tok_h, psS_last, psL_rd = {}, {}, {}, {}, {}, {}

        def emit_head_loads(h):
            hbq = h % 2
            prev_last = (h - 2) * NQT + (NQT - 1)

            def fn(s, h=h, hbq=hbq, w=qcp_tok.get((h - 2, NQT - 1), 0),
                   pw=pv_tok.get(prev_last, 0)):
                if w:
                    s.wait_ge(s_ve.h, w)
                srcq = sq[:, 256 * h:256 * (h + 1)].rearrange(
                    "(c p) u -> p c u", p=128)
                dstq = qslab[hbq][:].rearrange("p (c u) -> p c u", u=256)
                s.dma_start(out=dstq, in_=srcq).then_inc(sqs[hbq].h, 16)
                if pw:
                    s.wait_ge(s_mm.h, pw)
                s.dma_start(out=kstrip[hbq][:],
                            in_=sk[128 * h:128 * (h + 1), :]).then_inc(sks[hbq].h, 16)
            R["sync"].emit(fn)
            qs_toks[h] = sqs[hbq].inc(16)
            ks_toks[h] = sks[hbq].inc(16)

            def fn(g, h=h, hbq=hbq, tw=vtr_done.get(h - 2, 0)):
                if tw:
                    g.wait_ge(s_mm.h, tw)
                g.dma_start(out=vstr_dh[hbq][:],
                            in_=sv[128 * h:128 * (h + 1), :]).then_inc(svs[hbq].h, 16)
            R["gpsimd"].emit(fn)
            vs_toks[h] = svs[hbq].inc(16)

        def emit_qcp(nit):
            hq, qtq = divmod(nit, NQT)
            hbq, biq = hq % 2, nit % 2
            qcp_tok[(hq, qtq)] = s_ve.inc()

            def fn(v, w=psS_last.get(nit - 2, 0),
                   qw=qs_toks[hq] if qtq == 0 else 0,
                   hbq=hbq, biq=biq, qtq=qtq):
                if qw:
                    v.wait_ge(sqs[hbq].h, qw)
                if w:
                    v.wait_ge(s_mm.h, w)
                srcv = qslab[hbq][:].rearrange("p (j u) -> p u j", j=16)[
                    :, 32 * qtq:32 * (qtq + 1), :]
                dstv = qtile[biq][:].rearrange("p (u j) -> p u j", j=16)
                v.tensor_copy(dstv, srcv).then_inc(s_ve.h, 1)
            R["vector"].emit(fn)

        def emit_tail(pit, all_exp_p, mask_tok_p, acc_fin_p):
            hq, qtq = divmod(pit, NQT)
            hbq, biq = hq % 2, pit % 2
            kbm_p = 4 * (qtq + 1)

            def fn(tn, w1=acc_fin_p, kbm=kbm_p, bi=biq, hb=hbq,
                   vw=vtr_tok_h[hq] if qtq == 0 else 0,
                   w2=mask_tok_p,
                   lw=r_tok.get(pit - 1, 0), cw2=ctx_tok.get(pit - 1, 0),
                   cw=cst2 if pit == 0 else 0):
                if cw:
                    tn.wait_ge(s_cst.h, cw)
                if vw:
                    tn.wait_ge(s_ve.h, vw)
                tn.wait_ge(s_ve.h, w1)
                if lw:
                    tn.wait_ge(s_sc.h, lw)
                tn.matmul(psL[:], ones_sb[:], eaccb[bi][:],
                          start=True, stop=True).then_inc(s_mm.h, 1)
                if cw2:
                    tn.wait_ge(s_ve.h, cw2)
                for kb in range(kbm):
                    ins = tn.matmul(psC[:], vstrip[hb][:, 128 * kb:128 * (kb + 1)],
                              ebuf[bi][:, QW * kb:QW * (kb + 1)],
                              start=(kb == 0), stop=(kb == kbm - 1))
                ins.then_inc(s_mm.h, 1)
            R["tensor"].emit(fn)
            psL_rd[pit] = s_mm.inc()
            pv_tok[pit] = s_mm.inc()

            ln_mid = s_sc.inc()
            r_tok[pit] = s_sc.inc()

            def fn(sc, w=pv_tok[pit], vw=ctx_tok.get(pit - 1, 0), lt=ln_mid):
                sc.wait_ge(s_mm.h, w)
                if vw:
                    sc.wait_ge(s_ve.h, vw)
                sc.activation(lnl[:], psL[:], AF.Ln).then_inc(s_sc.h, 1)
                sc.wait_ge(s_sc.h, lt)
                sc.activation(rwide[:], lnl[:], AF.Exp, scale=-1.0).then_inc(s_sc.h, 1)
            R["scalar"].emit(fn)

            def fn(v, w=r_tok[pit], bi=biq, sw=ctx_sp.get(pit - 2, 0)):
                v.wait_ge(s_sc.h, w)
                if sw:
                    v.wait_ge(ssp[bi].h, sw)
                v.tensor_tensor(ctxstg[bi][:], psC[:], rwide[:],
                                ALU.mult).then_inc(s_ve.h, 1)
            R["vector"].emit(fn)
            ctx_tok[pit] = s_ve.inc()

            def fn(s, w=ctx_tok[pit], hq=hq, qtq=qtq, bi=biq):
                s.wait_ge(s_ve.h, w)
                s.dma_start(out=cfull[128 * hq:128 * (hq + 1), QW * qtq:QW * (qtq + 1)],
                            in_=ctxstg[bi][:]).then_inc(ssp[bi].h, 16)
            R["sync"].emit(fn)
            ctx_sp[pit] = ssp[biq].inc(16)

        emit_head_loads(0)
        emit_qcp(0)
        pending = None

        for hh in range(HPC):
            hb = hh % 2
            prev_last = (hh - 2) * NQT + (NQT - 1)
            vs_tok = vs_toks[hh] if hh in vs_toks else None
            if vs_tok is None:
                vs_tok = vs_toks[0]

            for kb2 in range(32):
                slot2 = kb2 % 2

                def fn(tn, kb2=kb2, slot2=slot2, hb=hb, w=vs_tok,
                       pw=pv_tok.get(prev_last, 0),
                       ew=vtr_evac.get((hh, kb2 - 2), 0)):
                    if kb2 == 0:
                        tn.wait_ge(svs[hb].h, w)
                        if pw:
                            tn.wait_ge(s_mm.h, pw)
                    if ew:
                        tn.wait_ge(s_ve.h, ew)
                    tn.matmul(psT2[:, 512 * slot2:512 * slot2 + 128],
                              vstr_dh[hb][:, 128 * kb2:128 * (kb2 + 1)], eye_sb[:],
                              start=True, stop=True).then_inc(s_mm.h, 1)
                R["tensor"].emit(fn)
                vmm_tok = s_mm.inc()

                def fn(v, w=vmm_tok, kb2=kb2, slot2=slot2, hb=hb):
                    v.wait_ge(s_mm.h, w)
                    v.tensor_copy(vstrip[hb][:, 128 * kb2:128 * (kb2 + 1)],
                                  psT2[:, 512 * slot2:512 * slot2 + 128]).then_inc(s_ve.h, 1)
                R["vector"].emit(fn)
                vtr_evac[(hh, kb2)] = s_ve.inc()
            vtr_done[hh] = s_mm.n
            vtr_tok_h[hh] = vtr_evac[(hh, 31)]

            for qt in range(NQT):
                kbm = 4 * (qt + 1)
                bi = it % 2

                pmax_prev = 4 * ((qt - 1) + 1) // 2 if qt > 0 else 4 * NQT // 2
                for p in range(kbm // 2):
                    grp = p % 2

                    def fn(tn, p=p, grp=grp, bi=bi, hb=hb,
                           kw=ks_toks[hh] if qt == 0 else 0,
                           qw=qcp_tok[(hh, qt)],
                           ew=exp_tok.get((it, p - 2), 0) or
                              (exp_tok.get((it - 1,
                                            pmax_prev - 2 + (pmax_prev + p) % 2), 0)
                               if p < 2 else 0)):
                        if p == 0:
                            if kw:
                                tn.wait_ge(sks[hb].h, kw)
                            tn.wait_ge(s_ve.h, qw)
                        if ew:
                            tn.wait_ge(s_sc.h, ew)
                        for j in range(2):
                            kb = 2 * p + j
                            ins = tn.matmul(psS[:, 1024 * grp + 512 * j:1024 * grp + 512 * (j + 1)],
                                      kstrip[hb][:, 128 * kb:128 * (kb + 1)],
                                      qtile[bi][:], start=True, stop=True)
                        ins.then_inc(s_mm.h, 1)
                    R["tensor"].emit(fn)
                    mm_tok = s_mm.inc()

                    def fn(sc, w=mm_tok, p=p, grp=grp, bi=bi,
                           evw=pv_tok.get(it - 2, 0)):
                        sc.wait_ge(s_mm.h, w)
                        if p == 0 and evw:
                            sc.wait_ge(s_mm.h, evw)
                        sc.activation(ebuf[bi][:, QW * 2 * p:QW * 2 * (p + 1)],
                                      psS[:, 1024 * grp:1024 * (grp + 1)],
                                      AF.Exp, bias=cv_sb[:, 1:2]).then_inc(s_sc.h, 1)
                    R["scalar"].emit(fn)
                    exp_tok[(it, p)] = s_sc.inc()
                psS_last[it] = mm_tok

                all_exp = s_sc.n

                prev_acc = 0
                for kb in range(kbm):
                    j = kb - 4 * qt
                    mt = s_ve.inc() if j >= 0 else 0
                    at = s_ve.inc()

                    def fn(v, w=exp_tok[(it, kb // 2)], kb=kb, j=j, bi=bi,
                           pa=prev_acc, kbm=kbm, mt=mt,
                           pw=psL_rd.get(it - 2, 0) if kb == 0 else 0,
                           cw=cst2 if it == 0 else 0):
                        if cw:
                            v.wait_ge(s_cst.h, cw)
                        v.wait_ge(s_sc.h, w)
                        if j >= 0:
                            v.tensor_tensor(ebuf[bi][:, QW * kb:QW * (kb + 1)],
                                            ebuf[bi][:, QW * kb:QW * (kb + 1)],
                                            masks_sb[:, QW * j:QW * (j + 1)],
                                            ALU.mult).then_inc(s_ve.h, 1)
                            v.wait_ge(s_ve.h, mt)
                        if pw:
                            v.wait_ge(s_mm.h, pw)
                        if pa:
                            v.wait_ge(s_ve.h, pa)
                        if kb == 0:
                            v.tensor_copy(eacc[bi][:],
                                          ebuf[bi][:, 0:QW]).then_inc(s_ve.h, 1)
                        else:
                            dst = eaccb[bi][:] if kb == kbm - 1 else eacc[bi][:]
                            v.tensor_tensor(dst, eacc[bi][:],
                                            ebuf[bi][:, QW * kb:QW * (kb + 1)],
                                            ALU.add).then_inc(s_ve.h, 1)
                    R["vector"].emit(fn)
                    prev_acc = at
                mask_tok = prev_acc
                acc_fin = prev_acc

                nit = it + 1
                if nit < HPC * NQT:
                    if nit % NQT == 0:
                        emit_head_loads(nit // NQT)
                    emit_qcp(nit)
                if pending is not None:
                    emit_tail(*pending)
                pending = (it, all_exp, mask_tok, acc_fin)

                it += 1
        emit_tail(*pending)

        def fn(s, ws=[(ssp[i].h, ssp[i].n) for i in range(2)]):
            for h, w in ws:
                s.wait_ge(h, w)
        R["sync"].emit(fn)
        barrier()
        p2.close()
        if STOP <= 2 or mode == "l1":
            cm.close()
            return _finish()


    gall = sb(nc.sbuf_tensor("gall", [128, 16 * 8], BF16))
    spre_f = sb(nc.sbuf_tensor("spre_f", [8, D], F32))
    spre_b = sb(nc.sbuf_tensor("spre_b", [8, D], BF16))

    if mode == "fused":
        p25 = contextlib.ExitStack()
        cblk = [p25.enter_context(nc.sbuf_tensor(f"cblk{i}", [128, T], BF16)) for i in range(2)]
        cbt = [p25.enter_context(nc.sbuf_tensor(f"cbt{i}", [128, TOK], BF16)) for i in range(2)]
        cbm = [p25.enter_context(nc.sbuf_tensor(f"cbm{i}", [128, TOK], BF16)) for i in range(2)]
        cbs = [p25.enter_context(nc.sbuf_tensor(f"cbs{i}", [128, TOK], BF16)) for i in range(2)]

        bl_tok = {}
        bsp_tok = {}
        for hc in range(HPC):
            bi = hc % 2

            def fn(s, hc=hc, bi=bi, w=bl_tok.get(hc - 2, 0)):
                if w:
                    s.wait_ge(s_ve.h, w)
                s.dma_start(out=cblk[bi][:],
                            in_=cfull[128 * hc:128 * (hc + 1), :]).then_inc(sld[bi].h, 16)
            R["sync"].emit(fn)
            ld = sld[bi].inc(16)

            t1 = s_ve.inc(); t2 = s_ve.inc(); t3 = s_ve.inc()
            bl_tok[hc] = s_ve.inc()

            def fn(v, w=ld, bi=bi, sw=bsp_tok.get(hc - 2, 0), t1=t1, t2=t2, t3=t3):
                v.wait_ge(sld[bi].h, w)
                if sw:
                    v.wait_ge(ssp[2 + bi].h, sw)
                v.tensor_scalar_mul(cbt[bi][:], cblk[bi][:, 0:TOK],
                                    npsel_sb[:, 0:1]).then_inc(s_ve.h, 1)
                v.wait_ge(s_ve.h, t1)
                v.scalar_tensor_tensor(cbm[bi][:], cblk[bi][:, TOK:T], psel_sb[:, 0:1],
                                       cbt[bi][:], ALU.mult, ALU.add).then_inc(s_ve.h, 1)
                v.wait_ge(s_ve.h, t2)
                v.tensor_scalar_mul(cbt[bi][:], cblk[bi][:, 0:TOK],
                                    psel_sb[:, 0:1]).then_inc(s_ve.h, 1)
                v.wait_ge(s_ve.h, t3)
                v.scalar_tensor_tensor(cbs[bi][:], cblk[bi][:, TOK:T], npsel_sb[:, 0:1],
                                       cbt[bi][:], ALU.mult, ALU.add).then_inc(s_ve.h, 1)
            R["vector"].emit(fn)

            def fn(s, w=bl_tok[hc], hc=hc, bi=bi):
                s.wait_ge(s_ve.h, w)
                s.dma_start(out=cmine_d[128 * hc:128 * (hc + 1), :],
                            in_=cbm[bi][:]).then_inc(ssp[2 + bi].h, 16)
                s.dma_start(out=csend[128 * hc:128 * (hc + 1), :],
                            in_=cbs[bi][:]).then_inc(ssp[2 + bi].h, 16)
            R["sync"].emit(fn)
            bsp_tok[hc] = ssp[2 + bi].inc(32)

        def fn(g, w2=ssp[2].n, w3=ssp[3].n):
            g.wait_ge(ssp[2].h, w2)
            g.wait_ge(ssp[3].h, w3)
            g.collective_compute(
                "AllGather", ALU.bypass,
                replica_groups=[[0, 1], [2, 3], [4, 5], [6, 7]],
                ins=[csend[:, :]],
                outs=[agout[:, :]],
            ).then_inc(s_cc.h, 16)
        R["gpsimd"].emit(fn)
        ag_tok = s_cc.inc(16)
        barrier()
        p25.close()
        if STOP <= 3:
            cm.close()
            return _finish()


    p3 = contextlib.ExitStack()
    p3sb = p3.enter_context
    wo_sb = p3sb(nc.sbuf_tensor("wo_sb", [128, 16 * D], BF16))
    cm_sb = p3sb(nc.sbuf_tensor("cm_sb", [128, 8 * TOK], BF16))
    cp_sb = p3sb(nc.sbuf_tensor("cp_sb", [128, 8 * TOK], BF16))
    aga = [p3sb(nc.sbuf_tensor("aga0", [128, TOK], BF16))]
    agb = [p3sb(nc.sbuf_tensor("agb0", [128, TOK], BF16))]
    attmp = [p3sb(nc.sbuf_tensor(f"attmp{i}", [128, D], F32)) for i in range(2)]
    x1tl = [p3sb(nc.sbuf_tensor(f"x1tl{i}", [128, D], F32)) for i in range(2)]
    xrb = [p3sb(nc.sbuf_tensor(f"xrb{i}", [128, D], F32)) for i in range(2)]
    stat3 = [p3sb(nc.sbuf_tensor(f"stat3_{i}", [128, 24], F32)) for i in range(2)]
    mv3 = [p3sb(nc.sbuf_tensor(f"mv3_{i}", [128, 2], F32)) for i in range(2)]
    sd3 = [p3sb(nc.sbuf_tensor(f"sd3_{i}", [128, 1], F32)) for i in range(2)]
    rstd3 = [p3sb(nc.sbuf_tensor(f"rstd3_{i}", [128, 1], F32)) for i in range(2)]
    psO = [p3.enter_context(nc.psum_tensor(f"psO{i}", [128, 2048], F32))
           for i in range(2)]

    wo_tok = load_weight_cast(wo_sb, wo_h, D, "wo", s_wo)

    _cm_src = cmine_d if mode == "fused" else cm_h

    def fn(s, w2=ssp[2].n, w3=ssp[3].n, _cm_src=_cm_src):
        s.wait_ge(ssp[2].h, w2)
        s.wait_ge(ssp[3].h, w3)
        src = _cm_src[:, :].rearrange("(c p) q -> p c q", p=128)
        dst = cm_sb[:].rearrange("p (c q) -> p c q", q=TOK)
        s.dma_start(out=dst, in_=src).then_inc(scm.h, 16)
    R["sync"].emit(fn)
    cmld_tok = scm.inc(16)

    if mode == "l2":
        def fn(s):
            srcp = cp_h[:, :].rearrange("(c p) q -> p c q", p=128)
            dstp = cp_sb[:].rearrange("p (c q) -> p c q", q=TOK)
            s.dma_start(out=dstp, in_=srcp).then_inc(sag.h, 16)
        R["sync"].emit(fn)
        cp_dma = sag.inc(16)
        ag_tok = 0

    pb_tok = {}
    for hc in range(HPC if mode == "fused" else 0):
        bi = 0

        def fn(s, hc=hc, bi=bi, w=ag_tok, bw=pb_tok.get(hc - 1, 0)):
            if hc == 0:
                s.wait_ge(s_cc.h, w)
            if bw:
                s.wait_ge(s_ve.h, bw)
            s.dma_start(out=aga[bi][:],
                        in_=agout[128 * hc:128 * (hc + 1), :]).then_inc(sag.h, 16)
            s.dma_start(out=agb[bi][:],
                        in_=agout[HALF + 128 * hc:HALF + 128 * (hc + 1), :]
                        ).then_inc(sag.h, 16)
        R["sync"].emit(fn)
        ld = sag.inc(32)

        ta = s_ve.inc()
        pb_tok[hc] = s_ve.inc()

        def fn(v, w=ld, hc=hc, bi=bi, ta=ta):
            v.wait_ge(sag.h, w)
            v.tensor_scalar_mul(aga[bi][:], aga[bi][:],
                                psel_sb[:, 0:1]).then_inc(s_ve.h, 1)
            v.wait_ge(s_ve.h, ta)
            v.scalar_tensor_tensor(cp_sb[:, TOK * hc:TOK * (hc + 1)], agb[bi][:],
                                   npsel_sb[:, 0:1], aga[bi][:],
                                   ALU.mult, ALU.add).then_inc(s_ve.h, 1)
        R["vector"].emit(fn)
    cp_tok = pb_tok[HPC - 1] if mode == "fused" else 0

    x1_done = {}
    x1_sp = {}
    o_evac = {}
    for tq in range(16):
        bi = tq % 2

        def fn(s, tq=tq, bi=bi, w=x1_done.get(tq - 2, 0)):
            if w:
                s.wait_ge(s_ve.h, w)
            s.dma_start(out=xrb[bi][:],
                        in_=xres[128 * tq:128 * (tq + 1), :]).then_inc(sx[bi].h, 16)
        R["sync"].emit(fn)
        xw = sx[bi].inc(16)

        def fn(tn, tq=tq, w1=cmld_tok, w2=cp_tok, w3=wo_tok,
               w4=cp_dma if mode == "l2" else 0,
               ew=o_evac.get(tq - 2, 0)):
            if tq == 0:
                tn.wait_ge(scm.h, w1)
                if w2:
                    tn.wait_ge(s_ve.h, w2)
                if w4:
                    tn.wait_ge(sag.h, w4)
                tn.wait_ge(s_wo.h, w3)
            if ew:
                tn.wait_ge(s_ve.h, ew)
            for dmb in range(4):
                for hcx in range(16):
                    src = cm_sb if hcx < 8 else cp_sb
                    hcl = hcx % 8
                    ins = tn.matmul(psO[tq % 2][:, 512 * dmb:512 * (dmb + 1)],
                              src[:, TOK * hcl + 128 * tq:TOK * hcl + 128 * (tq + 1)],
                              wo_sb[:, D * hcx + 512 * dmb:D * hcx + 512 * (dmb + 1)],
                              start=(hcx == 0), stop=(hcx == 15))
            ins.then_inc(s_mm.h, 1)
        R["tensor"].emit(fn)
        mm_tok = s_mm.inc()

        tadd = s_ve.inc(); tstat = s_ve.inc()
        o_evac[tq] = ve_tok = s_ve.inc()

        def fn(v, w=mm_tok, xw=x1_done.get(tq - 2, 0), tadd=tadd, tstat=tstat,
               tq=tq, bi=bi, cw=const_gld if tq == 0 else 0):
            if cw:
                v.wait_ge(s_cst.h, cw)
            v.wait_ge(s_mm.h, w)
            if xw:
                v.wait_ge(s_ve.h, xw)
            for dmb in range(4):
                ins = v.tensor_tensor(attmp[bi][:, 512 * dmb:512 * (dmb + 1)],
                                      psO[tq % 2][:, 512 * dmb:512 * (dmb + 1)],
                                      bob_sb[:, 512 * dmb:512 * (dmb + 1)], ALU.add)
            ins.then_inc(s_ve.h, 1)
            v.wait_ge(s_ve.h, tadd)
            for g4 in range(4):
                ins = v.bn_stats(stat3[bi][:, 6 * g4:6 * (g4 + 1)],
                                 attmp[bi][:, 512 * g4:512 * (g4 + 1)])
            ins.then_inc(s_ve.h, 1)
            v.wait_ge(s_ve.h, tstat)
            v.bn_aggr(mv3[bi][:], stat3[bi][:]).then_inc(s_ve.h, 1)
        R["vector"].emit(fn)

        def fn(sc, w=ve_tok, bi=bi, lw=const_ld if tq == 0 else 0):
            if lw:
                sc.wait_ge(s_ld.h, lw)
            sc.wait_ge(s_ve.h, w)
            sc.activation(sd3[bi][:], mv3[bi][:, 1:2], AF.Sqrt, bias=cv_sb[:, 0:1]).then_inc(s_sc.h, 1)
        R["scalar"].emit(fn)
        sc_tok = s_sc.inc()

        tr3 = s_ve.inc(); tn3 = s_ve.inc()
        x1_done[tq] = s_ve.inc()

        def fn(v, w=sc_tok, bi=bi, xw=xw, sw=x1_sp.get(tq - 2, (0, 0)),
               tr3=tr3, tn3=tn3):
            v.wait_ge(s_sc.h, w)
            v.wait_ge(sx[bi].h, xw)
            if sw[0]:
                v.wait_ge(ssp[4 + bi].h, sw[0])
            if sw[1]:
                v.wait_ge(sxs[bi].h, sw[1])
            v.reciprocal(rstd3[bi][:], sd3[bi][:]).then_inc(s_ve.h, 1)
            v.wait_ge(s_ve.h, tr3)
            v.tensor_scalar(attmp[bi][:], attmp[bi][:], mv3[bi][:, 0:1], rstd3[bi][:],
                            ALU.subtract, ALU.mult).then_inc(s_ve.h, 1)
            v.wait_ge(s_ve.h, tn3)
            v.tensor_add(x1tl[bi][:], attmp[bi][:], xrb[bi][:]).then_inc(s_ve.h, 1)
        R["vector"].emit(fn)

        def fn(s, w=x1_done[tq], tq=tq, bi=bi):
            s.wait_ge(s_ve.h, w)
            s.dma_start(out=out_h[128 * tq:128 * (tq + 1), :],
                        in_=x1tl[bi][:]).then_inc(ssp[4 + bi].h, 16)
        R["sync"].emit(fn)

        def fn(g, w=x1_done[tq], tq=tq, bi=bi):
            g.wait_ge(s_ve.h, w)
            g.dma_start(out=x1s[128 * tq:128 * (tq + 1), :],
                        in_=x1tl[bi][:]).then_inc(sxs[bi].h, 16)
        R["gpsimd"].emit(fn)
        ssp[4 + bi].inc(16)
        x1_sp[tq] = (ssp[4 + bi].n, sxs[bi].inc(16))

    barrier()
    p3.close()
    if STOP <= 4:
        cm.close()
        return _finish()

    p4a = contextlib.ExitStack()
    asb = p4a.enter_context
    w1t_sb = asb(nc.sbuf_tensor("w1t_sb", [128, 16 * D], BF16))
    wrt_sb = asb(nc.sbuf_tensor("wrt_sb", [128, 16 * 8], BF16))
    x1b = [asb(nc.sbuf_tensor(f"x1b{i}", [128, D], BF16)) for i in range(2)]
    y2b = [asb(nc.sbuf_tensor(f"y2b{i}", [128, D], BF16)) for i in range(2)]
    y2t = [asb(nc.sbuf_tensor(f"y2t{i}", [128, 16 * QW], BF16)) for i in range(2)]
    stat4 = [asb(nc.sbuf_tensor(f"stat4_{i}", [128, 24], F32)) for i in range(2)]
    mv4 = [asb(nc.sbuf_tensor(f"mv4_{i}", [128, 2], F32)) for i in range(2)]
    sd4 = [asb(nc.sbuf_tensor(f"sd4_{i}", [128, 1], F32)) for i in range(2)]
    rstd4 = [asb(nc.sbuf_tensor(f"rstd4_{i}", [128, 1], F32)) for i in range(2)]
    h1stg = [asb(nc.sbuf_tensor(f"h1stg{i}", [128, QW], BF16)) for i in range(2)]
    rew = asb(nc.sbuf_tensor("rew", [128, 8], F32))
    rm1 = asb(nc.sbuf_tensor("rm1", [128, 1], F32))
    rm2 = asb(nc.sbuf_tensor("rm2", [128, 1], F32))
    rd_ = asb(nc.sbuf_tensor("rd_", [128, 8], F32))
    rmask = asb(nc.sbuf_tensor("rmask", [128, 8], F32))
    rden = asb(nc.sbuf_tensor("rden", [128, 1], F32))
    rrec = asb(nc.sbuf_tensor("rrec", [128, 1], F32))
    psT4 = p4a.enter_context(nc.psum_tensor("psT4", [128, 1024], F32))
    psR = p4a.enter_context(nc.psum_tensor("psR", [128, 512], F32))
    psH = [p4a.enter_context(nc.psum_tensor(f"psH{i}", [128, 512], F32)) for i in range(2)]

    def gn(g, w4=ssp[4].n, w5=ssp[5].n, x4=sxs[0].n, x5=sxs[1].n):
        g.wait_ge(ssp[4].h, w4)
        g.wait_ge(ssp[5].h, w5)
        g.wait_ge(sxs[0].h, x4)
        g.wait_ge(sxs[1].h, x5)
    R["gpsimd"].emit(gn)
    load_weight_cast(w1t_sb, w1t_h, D, "w1t", s_w1)

    def gn(g):
        src = wrt_h[:, :].rearrange("(c p) o -> p c o", p=128)
        dst = wrt_sb[:].rearrange("p (c o) -> p c o", o=8)
        g.dma_start(out=dst, in_=src).then_inc(s_w1.h, 16)
    R["gpsimd"].emit(gn)
    w1_tok = s_w1.inc(16)

    y2t_free = [0, 0]
    ln2_done = {}
    tr_evac4 = {}
    ytr2_done = {}
    rexp_tok = {}
    rg_tok = {}
    h1_si = {}
    h1_sp = {"a": [0, 0]}
    for qt4 in range(4):
        y2i = qt4 % 2
        for s4 in range(4):
            tq = 4 * qt4 + s4
            bi = tq % 2

            def fn(s, tq=tq, bi=bi, w=ln2_done.get(tq - 2, 0),
                   w4=sxs[0].n if tq < 2 else 0, w5=sxs[1].n if tq < 2 else 0):
                if w4:
                    s.wait_ge(sxs[0].h, w4)
                    s.wait_ge(sxs[1].h, w5)
                if w:
                    s.wait_ge(s_ve.h, w)
                s.dma_start(out=x1b[bi][:],
                            in_=x1s[128 * tq:128 * (tq + 1), :]).then_inc(sx[bi].h, 16)
            R["sync"].emit(fn)
            xw = sx[bi].inc(16)

            bnst4 = s_ve.inc()
            bn_tok = s_ve.inc()

            def fn(v, w=xw, bi=bi, st=bnst4):
                v.wait_ge(sx[bi].h, w)
                for g4 in range(4):
                    ins = v.bn_stats(stat4[bi][:, 6 * g4:6 * (g4 + 1)],
                                     x1b[bi][:, 512 * g4:512 * (g4 + 1)])
                ins.then_inc(s_ve.h, 1)
                v.wait_ge(s_ve.h, st)
                v.bn_aggr(mv4[bi][:], stat4[bi][:]).then_inc(s_ve.h, 1)
            R["vector"].emit(fn)

            def fn(sc, w=bn_tok, bi=bi):
                sc.wait_ge(s_ve.h, w)
                sc.activation(sd4[bi][:], mv4[bi][:, 1:2], AF.Sqrt, bias=cv_sb[:, 0:1]).then_inc(s_sc.h, 1)
            R["scalar"].emit(fn)
            sq4_tok = s_sc.inc()

            rcp4 = s_ve.inc()
            ln2_done[tq] = s_ve.inc()

            def fn(v, w=sq4_tok, bi=bi, tw=ytr2_done.get(tq - 2, 0), rt=rcp4):
                v.wait_ge(s_sc.h, w)
                if tw:
                    v.wait_ge(s_mm.h, tw)
                v.reciprocal(rstd4[bi][:], sd4[bi][:]).then_inc(s_ve.h, 1)
                v.wait_ge(s_ve.h, rt)
                v.tensor_scalar(y2b[bi][:], x1b[bi][:], mv4[bi][:, 0:1], rstd4[bi][:],
                                ALU.subtract, ALU.mult).then_inc(s_ve.h, 1)
            R["vector"].emit(fn)

            for c in range(16):
                slot = c % 2

                def fn(tn, bi=bi, c=c, slot=slot, w=ln2_done[tq],
                       ew=tr_evac4.get((tq, c - 2), 0),
                       cw=const_gld if (tq == 0 and c == 0) else 0):
                    if cw:
                        tn.wait_ge(s_cst.h, cw)
                    if c == 0:
                        tn.wait_ge(s_ve.h, w)
                    if ew:
                        tn.wait_ge(s_ve.h, ew)
                    tn.matmul(psT4[:, 512 * slot:512 * slot + 128],
                              y2b[bi][:, 128 * c:128 * (c + 1)], eye_sb[:],
                              start=True, stop=True).then_inc(s_mm.h, 1)
                R["tensor"].emit(fn)
                mm_tok = s_mm.inc()

                def fn(v, w=mm_tok, c=c, slot=slot, s4=s4, y2i=y2i,
                       fw=y2t_free[y2i] if (s4 == 0 and c == 0) else 0):
                    v.wait_ge(s_mm.h, w)
                    if fw:
                        v.wait_ge(s_mm.h, fw)
                    v.tensor_copy(
                        y2t[y2i][:, 512 * c + 128 * s4:512 * c + 128 * (s4 + 1)],
                        psT4[:, 512 * slot:512 * slot + 128]).then_inc(s_ve.h, 1)
                R["vector"].emit(fn)
                tr_evac4[(tq, c)] = s_ve.inc()
            ytr2_done[tq] = s_mm.n
            tp_tok = tr_evac4[(tq, 15)]

            def fn(tn, w=tp_tok, wg=w1_tok, s4=s4, y2i=y2i, tq=tq,
                   pw=rexp_tok.get(tq - 1, 0)):
                tn.wait_ge(s_ve.h, w)
                if tq == 0:
                    tn.wait_ge(s_w1.h, wg)
                if pw:
                    tn.wait_ge(s_sc.h, pw)
                for dc in range(16):
                    ins = tn.matmul(psR[:, 0:8],
                              y2t[y2i][:, 512 * dc + 128 * s4:512 * dc + 128 * (s4 + 1)],
                              wrt_sb[:, 8 * dc:8 * (dc + 1)],
                              start=(dc == 0), stop=(dc == 15))
                ins.then_inc(s_mm.h, 1)
            R["tensor"].emit(fn)
            rmm = s_mm.inc()

            def fn(v, w=rmm, gw=rg_tok.get(tq - 1, 0)):
                v.wait_ge(s_mm.h, w)
                if gw:
                    v.wait_ge(s_ve.h, gw)
                v.reduce_max(rm1[:], psR[:, 0:8], axis=AX.X, negate=True
                             ).then_inc(s_ve.h, 1)
            R["vector"].emit(fn)
            nm_tok = s_ve.inc()

            def fn(sc, w=nm_tok):
                sc.wait_ge(s_ve.h, w)
                sc.activation(rew[:], psR[:, 0:8], AF.Exp, bias=rm1[:]
                              ).then_inc(s_sc.h, 1)
            R["scalar"].emit(fn)
            rexp_tok[tq] = s_sc.inc()

            rtoks = [s_ve.inc() for _ in range(10)]
            rg_tok[tq] = s_ve.inc()

            def fn(v, w=rexp_tok[tq], tq=tq, rtoks=rtoks):
                v.wait_ge(s_sc.h, w)
                v.reduce_max(rm1[:], rew[:], axis=AX.X).then_inc(s_ve.h, 1)
                v.wait_ge(s_ve.h, rtoks[0])
                v.tensor_scalar(rd_[:], rew[:], rm1[:], None,
                                ALU.subtract).then_inc(s_ve.h, 1)
                v.wait_ge(s_ve.h, rtoks[1])
                v.tensor_scalar(rmask[:], rd_[:], 0.0, None,
                                ALU.is_ge).then_inc(s_ve.h, 1)
                v.wait_ge(s_ve.h, rtoks[2])
                v.scalar_tensor_tensor(rd_[:], rmask[:], -1e30, rew[:],
                                       ALU.mult, ALU.add).then_inc(s_ve.h, 1)
                v.wait_ge(s_ve.h, rtoks[3])
                v.reduce_max(rm2[:], rd_[:], axis=AX.X).then_inc(s_ve.h, 1)
                v.wait_ge(s_ve.h, rtoks[4])
                v.tensor_scalar(rd_[:], rew[:], rm2[:], None,
                                ALU.subtract).then_inc(s_ve.h, 1)
                v.wait_ge(s_ve.h, rtoks[5])
                v.tensor_scalar(rmask[:], rd_[:], 0.0, None,
                                ALU.is_ge).then_inc(s_ve.h, 1)
                v.wait_ge(s_ve.h, rtoks[6])
                v.tensor_tensor(rd_[:], rew[:], rmask[:],
                                ALU.mult).then_inc(s_ve.h, 1)
                v.wait_ge(s_ve.h, rtoks[7])
                v.tensor_tensor(rden[:], rm1[:], rm2[:], ALU.add).then_inc(s_ve.h, 1)
                v.wait_ge(s_ve.h, rtoks[8])
                v.reciprocal(rrec[:], rden[:]).then_inc(s_ve.h, 1)
                v.wait_ge(s_ve.h, rtoks[9])
                v.tensor_scalar(gall[:, 8 * tq:8 * (tq + 1)], rd_[:], rrec[:],
                                None, ALU.mult).then_inc(s_ve.h, 1)
            R["vector"].emit(fn)

        tpw = tr_evac4[(4 * qt4 + 3, 15)]
        for ic in range(16):
            pi = ic % 2

            def fn(tn, w=tpw, ic=ic, pi=pi, y2i=y2i,
                   ew=h1_si.get((qt4, ic - 2), 0) or
                      (h1_si.get((qt4 - 1, 14 + (ic)), 0) if ic < 2 else 0)):
                if ic == 0:
                    tn.wait_ge(s_ve.h, w)
                if ew:
                    tn.wait_ge(s_sc.h, ew)
                for dc in range(16):
                    ins = tn.matmul(psH[pi][:],
                              w1t_sb[:, D * dc + 128 * ic:D * dc + 128 * (ic + 1)],
                              y2t[y2i][:, 512 * dc:512 * (dc + 1)],
                              start=(dc == 0), stop=(dc == 15))
                ins.then_inc(s_mm.h, 1)
            R["tensor"].emit(fn)
            mm_tok = s_mm.inc()

            def fn(sc, w=mm_tok, pi=pi, sw=h1_sp["a"][pi]):
                sc.wait_ge(s_mm.h, w)
                if sw:
                    sc.wait_ge(ssp[pi].h, sw)
                sc.activation(h1stg[pi][:], psH[pi][:], AF.Silu).then_inc(s_sc.h, 1)
            R["scalar"].emit(fn)
            h1_si[(qt4, ic)] = si_tok = s_sc.inc()

            def fn(s, w=si_tok, ic=ic, qt4=qt4, pi=pi):
                s.wait_ge(s_sc.h, w)
                s.dma_start(out=sh1[128 * ic:128 * (ic + 1), QW * qt4:QW * (qt4 + 1)],
                            in_=h1stg[pi][:]).then_inc(ssp[pi].h, 16)
            R["sync"].emit(fn)
            h1_sp["a"][pi] = ssp[pi].inc(16)

        y2t_free[y2i] = s_mm.n

    def fn(s, w0=ssp[0].n, w1=ssp[1].n):
        s.wait_ge(ssp[0].h, w0)
        s.wait_ge(ssp[1].h, w1)
    R["sync"].emit(fn)
    barrier()
    p4a.close()
    if STOP <= 5:
        cm.close()
        return _finish()

    p4b = contextlib.ExitStack()
    bsb = p4b.enter_context
    w3t_sb = bsb(nc.sbuf_tensor("w3t_sb", [128, 16 * D], BF16))
    h1str = [bsb(nc.sbuf_tensor(f"h1str{i}", [128, 16 * 128], BF16)) for i in range(2)]
    h2b = [bsb(nc.sbuf_tensor(f"h2b{i}", [128, D], BF16)) for i in range(2)]
    psH2 = [p4b.enter_context(nc.psum_tensor(f"psH2{i}", [128, 512], F32)) for i in range(2)]
    psSp = p4b.enter_context(nc.psum_tensor("psSp", [8, 2048], F32))

    def gn(g, w0=ssp[0].n, w1=ssp[1].n):
        g.wait_ge(ssp[0].h, w0)
        g.wait_ge(ssp[1].h, w1)
    R["gpsimd"].emit(gn)
    w3_tok = load_weight_cast(w3t_sb, w3t_h, D, "w3t", s_w3)

    sp_done = {}
    h2_si = {}
    for tq in range(16):
        bi = tq % 2

        def fn(s, tq=tq, bi=bi, w=sp_done.get(tq - 2, 0)):
            if w:
                s.wait_ge(s_mm.h, w)
            src = sh1[:, 128 * tq:128 * (tq + 1)].rearrange(
                "(c p) q -> p c q", p=128)
            dst = h1str[bi][:].rearrange("p (c q) -> p c q", q=128)
            s.dma_start(out=dst, in_=src).then_inc(sld[bi].h, 16)
        R["sync"].emit(fn)
        ld = sld[bi].inc(16)

        for jt in range(4):
            pi = jt % 2

            def fn(tn, w=ld, wg=w3_tok, jt=jt, pi=pi, bi=bi, tq=tq,
                   ew=h2_si.get((tq, jt - 2), 0) or
                      (h2_si.get((tq - 1, 2 + jt), 0) if jt < 2 else 0)):
                if jt == 0:
                    tn.wait_ge(sld[bi].h, w)
                    if tq == 0:
                        tn.wait_ge(s_w3.h, wg)
                if ew:
                    tn.wait_ge(s_sc.h, ew)
                for ic in range(16):
                    ins = tn.matmul(psH2[pi][:],
                              h1str[bi][:, 128 * ic:128 * (ic + 1)],
                              w3t_sb[:, D * ic + 512 * jt:D * ic + 512 * (jt + 1)],
                              start=(ic == 0), stop=(ic == 15))
                ins.then_inc(s_mm.h, 1)
            R["tensor"].emit(fn)
            mm_tok = s_mm.inc()

            def fn(sc, w=mm_tok, jt=jt, pi=pi, bi=bi, tq=tq,
                   hw=sp_done.get(tq - 2, 0)):
                sc.wait_ge(s_mm.h, w)
                if jt == 0 and hw:
                    sc.wait_ge(s_mm.h, hw)
                sc.activation(h2b[bi][:, 512 * jt:512 * (jt + 1)], psH2[pi][:],
                              AF.Silu).then_inc(s_sc.h, 1)
            R["scalar"].emit(fn)
            h2_si[(tq, jt)] = s_sc.inc()
        h2done = h2_si[(tq, 3)]

        def fn(tn, w=h2done, tq=tq, bi=bi, rgw=rg_tok[tq]):
            tn.wait_ge(s_sc.h, w)
            tn.wait_ge(s_ve.h, rgw)
            for jt in range(4):
                ins = tn.matmul(psSp[:, 512 * jt:512 * (jt + 1)],
                          gall[:, 8 * tq:8 * (tq + 1)],
                          h2b[bi][:, 512 * jt:512 * (jt + 1)],
                          start=(tq == 0), stop=(tq == 15))
            ins.then_inc(s_mm.h, 1)
        R["tensor"].emit(fn)
        sp_done[tq] = s_mm.inc()

    tspf = s_ve.inc()
    spre_tok = s_ve.inc()

    def fn(v, w=sp_done[15], tspf=tspf):
        v.wait_ge(s_mm.h, w)
        v.tensor_copy(spre_f[:], psSp[:]).then_inc(s_ve.h, 1)
        v.wait_ge(s_ve.h, tspf)
        v.tensor_copy(spre_b[:], spre_f[:]).then_inc(s_ve.h, 1)
    R["vector"].emit(fn)

    if mode == "l2":
        def fn(s, w=spre_tok):
            s.wait_ge(s_ve.h, w)
            s.dma_start(out=spre_h[:, :], in_=spre_f[:]).then_inc(s_st.h, 16)
        R["sync"].emit(fn)
        spre_st = s_st.inc(16)

        def fn(s, w=spre_st, w4=ssp[4].n, w5=ssp[5].n):
            s.wait_ge(ssp[4].h, w4)
            s.wait_ge(ssp[5].h, w5)
            s.wait_ge(s_st.h, w)
        R["sync"].emit(fn)
        p4b.close()
        cm.close()
        return _finish()

    barrier()
    p4b.close()
    if STOP <= 6:
        cm.close()
        return _finish()

    p4c = contextlib.ExitStack()
    csb = p4c.enter_context
    w2t_sb = csb(nc.sbuf_tensor("w2t_sb", [128, 16 * D], BF16))
    spreT = csb(nc.sbuf_tensor("spreT", [128, 16 * 8], BF16))
    ssum_st = csb(nc.sbuf_tensor("ssum_st", [8, D], F32))
    Ssb = csb(nc.sbuf_tensor("Ssb", [8, D], F32))
    statS = csb(nc.sbuf_tensor("statS", [8, 24], F32))
    mvS = csb(nc.sbuf_tensor("mvS", [8, 2], F32))
    sdS = csb(nc.sbuf_tensor("sdS", [8, 1], F32))
    rstdS = csb(nc.sbuf_tensor("rstdS", [8, 1], F32))
    x18 = csb(nc.sbuf_tensor("x18", [8, D], F32))
    fix8 = csb(nc.sbuf_tensor("fix8", [8, D], F32))
    psT5 = p4c.enter_context(nc.psum_tensor("psT5", [128, 512], F32))
    psS2 = p4c.enter_context(nc.psum_tensor("psS2", [8, 2048], F32))

    w2_tok = load_weight_cast(w2t_sb, w2t_h, D, "w2t", s_w2)

    t5_evac = {}
    for t16 in range(16):
        def fn(tn, w=spre_tok, t16=t16, ew=t5_evac.get(t16 - 1, 0)):
            if t16 == 0:
                tn.wait_ge(s_ve.h, w)
            if ew:
                tn.wait_ge(s_ve.h, ew)
            tn.matmul(psT5[:, 0:8],
                      spre_b[:, 128 * t16:128 * (t16 + 1)], eye_sb[0:8, 0:8],
                      start=True, stop=True).then_inc(s_mm.h, 1)
        R["tensor"].emit(fn)
        mm_tok = s_mm.inc()

        def fn(v, w=mm_tok, t16=t16):
            v.wait_ge(s_mm.h, w)
            v.tensor_copy(spreT[:, 8 * t16:8 * (t16 + 1)],
                          psT5[:, 0:8]).then_inc(s_ve.h, 1)
        R["vector"].emit(fn)
        t5_evac[t16] = s_ve.inc()

    def fn(tn, w=t5_evac[15], wg=w2_tok):
        tn.wait_ge(s_ve.h, w)
        tn.wait_ge(s_w2.h, wg)
        for dt in range(4):
            for ic in range(16):
                ins = tn.matmul(psS2[:, 512 * dt:512 * (dt + 1)],
                          spreT[:, 8 * ic:8 * (ic + 1)],
                          w2t_sb[:, D * ic + 512 * dt:D * ic + 512 * (dt + 1)],
                          start=(ic == 0), stop=(ic == 15))
        ins.then_inc(s_mm.h, 1)
    R["tensor"].emit(fn)
    smm_tok = s_mm.inc()

    def fn(v, w=smm_tok):
        v.wait_ge(s_mm.h, w)
        v.tensor_copy(ssum_st[:], psS2[:]).then_inc(s_ve.h, 1)
    R["vector"].emit(fn)
    sst_tok = s_ve.inc()

    def fn(s, w=sst_tok):
        s.wait_ge(s_ve.h, w)
        s.dma_start(out=ssum_in[:, :], in_=ssum_st[:]).then_inc(ssp[2].h, 16)
    R["sync"].emit(fn)
    ssin_tok = ssp[2].inc(16)

    def fn(g, w=ssin_tok):
        g.wait_ge(ssp[2].h, w)
        g.collective_compute(
            "AllReduce", ALU.add,
            replica_groups=[[0, 1, 2, 3, 4, 5, 6, 7]],
            ins=[ssum_in[:, :]],
            outs=[ssum_out[:, :]],
        ).then_inc(s_cc.h, 16)
    R["gpsimd"].emit(fn)
    ar_tok = s_cc.inc(16)

    def fn(s, w=ar_tok):
        s.wait_ge(s_cc.h, w)
        s.dma_start(out=Ssb[:], in_=ssum_out[:, :]).then_inc(scm.h, 16)
        s.dma_start(out=x18[:], in_=x1s[0:8, :]).then_inc(scm.h, 16)
    R["sync"].emit(fn)
    fin_ld = scm.inc(32)

    tstS = s_ve.inc()
    bnS_tok = s_ve.inc()

    def fn(v, w=fin_ld, tstS=tstS):
        v.wait_ge(scm.h, w)
        for g4 in range(4):
            ins = v.bn_stats(statS[:, 6 * g4:6 * (g4 + 1)],
                             Ssb[:, 512 * g4:512 * (g4 + 1)])
        ins.then_inc(s_ve.h, 1)
        v.wait_ge(s_ve.h, tstS)
        v.bn_aggr(mvS[:], statS[:]).then_inc(s_ve.h, 1)
    R["vector"].emit(fn)

    def fn(sc, w=bnS_tok):
        sc.wait_ge(s_ve.h, w)
        sc.activation(sdS[:], mvS[:, 1:2], AF.Sqrt,
                      bias=cv_sb[0:8, 0:1]).then_inc(s_sc.h, 1)
    R["scalar"].emit(fn)
    sqS_tok = s_sc.inc()

    tf = [s_ve.inc() for _ in range(3)]
    fix_tok = s_ve.inc()

    def fn(v, w=sqS_tok, tf=tf):
        v.wait_ge(s_sc.h, w)
        v.reciprocal(rstdS[:], sdS[:]).then_inc(s_ve.h, 1)
        v.wait_ge(s_ve.h, tf[0])
        v.tensor_scalar(Ssb[:], Ssb[:], mvS[:, 0:1], rstdS[:],
                        ALU.subtract, ALU.mult).then_inc(s_ve.h, 1)
        v.wait_ge(s_ve.h, tf[1])
        v.tensor_scalar(Ssb[:], Ssb[:], gate_sb[0:8, 0:1], None,
                        ALU.mult).then_inc(s_ve.h, 1)
        v.wait_ge(s_ve.h, tf[2])
        v.tensor_add(fix8[:], x18[:], Ssb[:]).then_inc(s_ve.h, 1)
    R["vector"].emit(fn)

    def fn(s, w=fix_tok):
        s.wait_ge(s_ve.h, w)
        s.dma_start(out=out_h[0:8, :], in_=fix8[:]).then_inc(s_st.h, 16)
    R["sync"].emit(fn)
    final_st = s_st.inc(16)

    def fn(s, w=final_st, w4=ssp[4].n, w5=ssp[5].n):
        s.wait_ge(ssp[4].h, w4)
        s.wait_ge(ssp[5].h, w5)
        s.wait_ge(s_st.h, w)
    R["sync"].emit(fn)

    p4c.close()
    cm.close()
    return _finish()


def build_nc3():
    nc = bass.Bass("TRN2", target_bir_lowering=False)
    spre_all = nc.dram_tensor("spre_all", [8, 8 * D], F32, kind="ExternalInput").ap()
    x18_h = nc.dram_tensor("x18", [8, D], F32, kind="ExternalInput").ap()
    w2t_h = nc.dram_tensor("w2t", [D, D], F32, kind="ExternalInput").ap()
    eye_h = nc.dram_tensor("eye", [128, 128], F32, kind="ExternalInput").ap()
    cvec_h = nc.dram_tensor("cvec", [128, 2], F32, kind="ExternalInput").ap()
    out8_h = nc.dram_tensor("out8", [8, D], F32, kind="ExternalOutput").ap()

    R = {k: Rec() for k in ENG}
    cm = contextlib.ExitStack()

    def _finish():
        with nc.Block() as block:
            @block.sync
            def _(e):
                R["sync"].run(e)

            @block.tensor
            def _(e):
                R["tensor"].run(e)

            @block.vector
            def _(e):
                R["vector"].run(e)

            @block.scalar
            def _(e):
                R["scalar"].run(e)

            @block.gpsimd
            def _(e):
                R["gpsimd"].run(e)
        return nc

    def mksem(name):
        return Sem(cm.enter_context(nc.semaphore(name)))

    s_ld = mksem("s_ld")
    s_gld = mksem("s_gld")
    s_mm = mksem("s_mm")
    s_ve = mksem("s_ve")
    s_sc = mksem("s_sc")
    s_st = mksem("s_st")
    sb = cm.enter_context

    sp_all = sb(nc.sbuf_tensor("sp_all", [8, 8 * D], F32))
    tmp4 = sb(nc.sbuf_tensor("tmp4", [8, 4 * D], F32))
    ssum = sb(nc.sbuf_tensor("ssum", [8, D], F32))
    ssum_b = sb(nc.sbuf_tensor("ssum_b", [8, D], BF16))
    spreT = sb(nc.sbuf_tensor("spreT", [128, 16 * 8], BF16))
    eye_sb = sb(nc.sbuf_tensor("eye_sb", [128, 128], BF16))
    cv_sb = sb(nc.sbuf_tensor("cv_sb", [128, 2], F32))
    w2t_sb = sb(nc.sbuf_tensor("w2t_sb", [128, 16 * D], BF16))
    Ssb = sb(nc.sbuf_tensor("Ssb", [8, D], F32))
    statS = sb(nc.sbuf_tensor("statS", [8, 24], F32))
    mvS = sb(nc.sbuf_tensor("mvS", [8, 2], F32))
    sdS = sb(nc.sbuf_tensor("sdS", [8, 1], F32))
    rstdS = sb(nc.sbuf_tensor("rstdS", [8, 1], F32))
    x18 = sb(nc.sbuf_tensor("x18_sb", [8, D], F32))
    fix8 = sb(nc.sbuf_tensor("fix8", [8, D], F32))
    psT5 = cm.enter_context(nc.psum_tensor("psT5", [128, 512], F32))
    psS2 = cm.enter_context(nc.psum_tensor("psS2", [8, 2048], F32))

    def fn(s):
        s.dma_start(out=sp_all[:], in_=spre_all[:, :]).then_inc(s_ld.h, 16)
        s.dma_start(out=x18[:], in_=x18_h[:, :]).then_inc(s_ld.h, 16)
        s.dma_start(out=cv_sb[:], in_=cvec_h[:, :]).then_inc(s_ld.h, 16)
    R["sync"].emit(fn)
    ld_tok = s_ld.inc(48)

    def fn(g):
        g.dma_start(out=eye_sb[:], in_=eye_h[:, :]).then_inc(s_gld.h, 16)
        for c4 in range(4):
            dst = w2t_sb[:, 4 * D * c4: 4 * D * (c4 + 1)].rearrange(
                "p (c o) -> p c o", o=D)
            srcw = w2t_h[512 * c4:512 * (c4 + 1), :].rearrange(
                "(c p) o -> p c o", p=128)
            g.dma_start(out=dst, in_=srcw).then_inc(s_gld.h, 16)
    R["gpsimd"].emit(fn)
    gld_tok = s_gld.inc(80)

    lv1 = s_ve.inc(); lv2 = s_ve.inc(); sum_tok = s_ve.inc()

    def fn(v, w=ld_tok, lv1=lv1, lv2=lv2):
        v.wait_ge(s_ld.h, w)
        for j in range(4):
            ins = v.tensor_add(tmp4[:, D * j:D * (j + 1)],
                               sp_all[:, D * (2 * j):D * (2 * j + 1)],
                               sp_all[:, D * (2 * j + 1):D * (2 * j + 2)])
        ins.then_inc(s_ve.h, 1)
        v.wait_ge(s_ve.h, lv1)
        for j in range(2):
            ins = v.tensor_add(tmp4[:, D * (2 * j):D * (2 * j + 1)] if False else
                               sp_all[:, D * j:D * (j + 1)],
                               tmp4[:, D * (2 * j):D * (2 * j + 1)],
                               tmp4[:, D * (2 * j + 1):D * (2 * j + 2)])
        ins.then_inc(s_ve.h, 1)
        v.wait_ge(s_ve.h, lv2)
        v.tensor_add(ssum[:], sp_all[:, 0:D], sp_all[:, D:2 * D]).then_inc(s_ve.h, 1)
        v.wait_ge(s_ve.h, sum_tok)
        v.tensor_copy(ssum_b[:], ssum[:]).then_inc(s_ve.h, 1)
    R["vector"].emit(fn)
    cp_tok = s_ve.inc()

    t5_evac = {}
    for t16 in range(16):
        def fn(tn, w=cp_tok, t16=t16, ew=t5_evac.get(t16 - 1, 0),
               gw=gld_tok if t16 == 0 else 0):
            if t16 == 0:
                tn.wait_ge(s_ve.h, w)
                tn.wait_ge(s_gld.h, gw)
            if ew:
                tn.wait_ge(s_ve.h, ew)
            tn.matmul(psT5[:, 0:8],
                      ssum_b[:, 128 * t16:128 * (t16 + 1)], eye_sb[0:8, 0:8],
                      start=True, stop=True).then_inc(s_mm.h, 1)
        R["tensor"].emit(fn)
        mm_tok = s_mm.inc()

        def fn(v, w=mm_tok, t16=t16):
            v.wait_ge(s_mm.h, w)
            v.tensor_copy(spreT[:, 8 * t16:8 * (t16 + 1)],
                          psT5[:, 0:8]).then_inc(s_ve.h, 1)
        R["vector"].emit(fn)
        t5_evac[t16] = s_ve.inc()

    def fn(tn, w=t5_evac[15]):
        tn.wait_ge(s_ve.h, w)
        for dt in range(4):
            for ic in range(16):
                ins = tn.matmul(psS2[:, 512 * dt:512 * (dt + 1)],
                          spreT[:, 8 * ic:8 * (ic + 1)],
                          w2t_sb[:, D * ic + 512 * dt:D * ic + 512 * (dt + 1)],
                          start=(ic == 0), stop=(ic == 15))
        ins.then_inc(s_mm.h, 1)
    R["tensor"].emit(fn)
    smm_tok = s_mm.inc()

    sc_tok0 = s_ve.inc()
    bnS_mid = s_ve.inc()
    bnS_tok = s_ve.inc()

    def fn(v, w=smm_tok, t0=sc_tok0, mid=bnS_mid):
        v.wait_ge(s_mm.h, w)
        v.tensor_copy(Ssb[:], psS2[:]).then_inc(s_ve.h, 1)
        v.wait_ge(s_ve.h, t0)
        for g4 in range(4):
            ins = v.bn_stats(statS[:, 6 * g4:6 * (g4 + 1)],
                             Ssb[:, 512 * g4:512 * (g4 + 1)])
        ins.then_inc(s_ve.h, 1)
        v.wait_ge(s_ve.h, mid)
        v.bn_aggr(mvS[:], statS[:]).then_inc(s_ve.h, 1)
    R["vector"].emit(fn)

    def fn(sc, w=bnS_tok):
        sc.wait_ge(s_ve.h, w)
        sc.activation(sdS[:], mvS[:, 1:2], AF.Sqrt,
                      bias=cv_sb[0:8, 0:1]).then_inc(s_sc.h, 1)
    R["scalar"].emit(fn)
    sqS_tok = s_sc.inc()

    tf0 = s_ve.inc(); tf1 = s_ve.inc()
    fix_tok = s_ve.inc()

    def fn(v, w=sqS_tok, tf0=tf0, tf1=tf1):
        v.wait_ge(s_sc.h, w)
        v.reciprocal(rstdS[:], sdS[:]).then_inc(s_ve.h, 1)
        v.wait_ge(s_ve.h, tf0)
        v.tensor_scalar(Ssb[:], Ssb[:], mvS[:, 0:1], rstdS[:],
                        ALU.subtract, ALU.mult).then_inc(s_ve.h, 1)
        v.wait_ge(s_ve.h, tf1)
        v.tensor_add(fix8[:], x18[:], Ssb[:]).then_inc(s_ve.h, 1)
    R["vector"].emit(fn)

    def fn(s, w=fix_tok):
        s.wait_ge(s_ve.h, w)
        s.dma_start(out=out8_h[:, :], in_=fix8[:]).then_inc(s_st.h, 16)
        s.wait_ge(s_st.h, 16)
    R["sync"].emit(fn)

    cm.close()
    return _finish()


def _host_inputs(inputs):
    x = np.asarray(inputs["L0_attn_x"], dtype=np.float32)
    Wq = np.asarray(inputs["L0_attn_Wq"], dtype=np.float32)
    bq = np.asarray(inputs["L0_attn_bq"], dtype=np.float32)
    Wk = np.asarray(inputs["L0_attn_Wk"], dtype=np.float32)
    bk = np.asarray(inputs["L0_attn_bk"], dtype=np.float32)
    Wv = np.asarray(inputs["L0_attn_Wv"], dtype=np.float32)
    bv = np.asarray(inputs["L0_attn_bv"], dtype=np.float32)
    Wo = np.asarray(inputs["L0_attn_Wo"], dtype=np.float32)
    bo = np.asarray(inputs["L0_attn_bo"], dtype=np.float32)
    w1 = np.asarray(inputs["L0_ffn_w1"], dtype=np.float32)
    w3 = np.asarray(inputs["L0_ffn_w3"], dtype=np.float32)
    w2 = np.asarray(inputs["L0_ffn_w2"], dtype=np.float32)
    wr = np.asarray(inputs["L0_ffn_wr"], dtype=np.float32)

    eye = np.eye(128, dtype=np.float32)
    ones128 = np.ones((128, 128), dtype=np.float32)
    masks = np.zeros((128, 4 * QW), dtype=np.float32)
    qs = np.arange(QW)
    ks = np.arange(128)
    for j in range(4):
        masks[:, QW * j:QW * (j + 1)] = (qs[None, :] >= 128 * j + ks[:, None])
    cvec = np.broadcast_to(
        np.array([EPS, -CSTAB], dtype=np.float32), (128, 2)).copy()

    w1t = np.ascontiguousarray(w1.T)
    w3t = np.ascontiguousarray(w3.T)
    w2t = np.ascontiguousarray(w2.T)
    wrt = np.ascontiguousarray(wr.T)

    maps1, maps2 = [], []
    for c in range(NCORE):
        b, h = c // 2, c % 2
        cols = slice(HALF * h, HALF * (h + 1))
        wo_r = np.ascontiguousarray(
            np.concatenate([Wo[HALF * h:HALF * (h + 1), :],
                            Wo[HALF * (1 - h):HALF * (2 - h), :]], axis=0))
        maps1.append({
            "xb": np.ascontiguousarray(x[b]),
            "xres": np.ascontiguousarray(x[b][TOK * h:TOK * (h + 1)]),
            "wq": np.ascontiguousarray(Wq),
            "wk": np.ascontiguousarray(Wk[:, cols]),
            "wv": np.ascontiguousarray(Wv[:, cols]),
            "bq": np.ascontiguousarray(bq.reshape(16, 128).T),
            "bk": np.ascontiguousarray(bk[cols].reshape(HPC, 128).T),
            "bv": np.ascontiguousarray(bv[cols].reshape(HPC, 128).T),
            "masks": masks,
            "ones128": ones128,
            "eye": eye,
            "cvec": cvec,
        })
        maps2.append({
            "xres": np.ascontiguousarray(x[b][TOK * h:TOK * (h + 1)]),
            "wo": wo_r,
            "bob": np.ascontiguousarray(np.broadcast_to(bo, (128, D))),
            "w1t": w1t,
            "w3t": w3t,
            "wrt": wrt,
            "eye": eye,
            "cvec": cvec,
        })
    return maps1, maps2, w2t, eye, cvec


def kernel(**inputs):
    maps1, maps2, w2t, eye, cvec = _host_inputs(inputs)

    nc1 = build_nc("l1")
    res1 = run_bass_kernel_spmd(nc1, maps1, core_ids=list(range(NCORE)))

    for c in range(NCORE):
        b, h = c // 2, c % 2
        partner = 2 * b + (1 - h)
        own = res1.results[c]["cfull"]
        other = res1.results[partner]["cfull"]
        maps2[c]["cm_h"] = np.ascontiguousarray(own[:, TOK * h:TOK * (h + 1)])
        maps2[c]["cp_h"] = np.ascontiguousarray(other[:, TOK * h:TOK * (h + 1)])

    nc2 = build_nc("l2")
    res2 = run_bass_kernel_spmd(nc2, maps2, core_ids=list(range(NCORE)))

    spre_all = np.stack([np.asarray(res2.results[c]["spre"], dtype=np.float32)
                         for c in range(NCORE)], axis=1).reshape(8, 8 * D)
    x18 = np.ascontiguousarray(
        np.asarray(res2.results[0]["out"], dtype=np.float32)[0:8, :])

    nc3 = build_nc3()
    res3 = run_bass_kernel_spmd(
        nc3, [{"spre_all": np.ascontiguousarray(spre_all), "x18": x18,
               "w2t": w2t, "eye": eye, "cvec": cvec}], core_ids=[0])

    out = np.empty((B, T, D), dtype=np.float32)
    for c in range(NCORE):
        b, h = c // 2, c % 2
        out[b, TOK * h:TOK * (h + 1), :] = res2.results[c]["out"]
    out[0, 0:8, :] = res3.results[0]["out8"]
    return out
